# version 02
# baseline (speedup 1.0000x reference)
import numpy as np
import ml_dtypes
from contextlib import ExitStack

import concourse.bass as bass
import concourse.mybir as mybir
from concourse.bass_utils import run_bass_kernel_spmd

F32 = mybir.dt.float32
BF16 = mybir.dt.bfloat16
AF = mybir.ActivationFunctionType
ALU = mybir.AluOpType
AX = mybir.AxisListType

NCORES = 8
D = 1024
EPS = 1e-5

ENGINES = ("sync", "tensor", "vector", "scalar", "gpsimd")


def f32c(x):
    return np.ascontiguousarray(x, dtype=np.float32)


def prep_blocks(wT, m=128):
    din, dout = wT.shape
    assert dout % m == 0, (din, dout)
    return np.ascontiguousarray(wT.reshape(din, dout // m, m).transpose(1, 0, 2))


class _EngineProxy:
    def __init__(self, prog, ename):
        self._p = prog
        self._e = ename

    def __getattr__(self, op):
        p, ename = self._p, self._e

        def rec(*args, **kw):
            p.ops.append((ename, op, args, kw))
        return rec


class Prog:
    def __init__(self, nc, ctx):
        self.nc = nc
        self.ctx = ctx
        self.ops = []
        self.DS = ctx.enter_context(nc.semaphore("DS"))
        self.DG = ctx.enter_context(nc.semaphore("DG"))
        self.CCS = ctx.enter_context(nc.semaphore("CCS"))
        self.SB = ctx.enter_context(nc.semaphore("SBAR"))
        self._consts = {}
        self._const_sb = ctx.enter_context(nc.sbuf_tensor("constpool", [128, 16], F32))
        self.psA = ctx.enter_context(nc.psum_tensor("g_psA", [128, 2048], F32))
        self.psB = ctx.enter_context(nc.psum_tensor("g_psB", [128, 2048], F32))
        self.t = _EngineProxy(self, "tensor")
        self.v = _EngineProxy(self, "vector")
        self.s = _EngineProxy(self, "scalar")
        self.g = _EngineProxy(self, "gpsimd")
        self.y = _EngineProxy(self, "sync")

    def uniq(self, name):
        self._nuid = getattr(self, "_nuid", 0) + 1
        return f"{name}__{self._nuid}"

    def sb(self, name, shape, dtype):
        return self.ctx.enter_context(self.nc.sbuf_tensor(self.uniq(name), list(shape), dtype))

    def psum(self, name, shape, dtype=F32):
        return self.ctx.enter_context(self.nc.psum_tensor(name, list(shape), dtype))

    def sb_scope(self):
        prog = self

        class _Scope:
            def __init__(self):
                self.st = ExitStack()

            def sb(self, name, shape, dtype):
                return self.st.enter_context(prog.nc.sbuf_tensor(prog.uniq(name), list(shape), dtype))

            def close(self):
                self.st.close()
        return _Scope()

    def cf(self, value, dtype=F32):
        key = (dtype, float(value))
        if key not in self._consts:
            self._consts[key] = len(self._consts)
        return float(value)

    def bar(self):
        self.ops.append(("BAR",))

    def dma(self, out, in_, engine="sync", **kw):
        self.ops.append(("DMA", engine, (out, in_), kw))

    def cc(self, kind, alu, groups, ins, outs):
        self.ops.append(("CC", "gpsimd", (kind, alu, groups, ins, outs), {}))

    def finalize(self):
        if self._consts:
            assert len(self._consts) <= 16
            csb = self._const_sb
            init_ops = []
            for (dtype, val), idx in self._consts.items():
                ap = csb[0:128, idx:idx + 1]
                self.nc.const_aps.register(dtype, val, ap)
                init_ops.append(("vector", "memset", (ap, val), {}))
            self.ops = init_ops + [("BAR",)] + self.ops
        stages = [{e: [] for e in ENGINES}]
        stage_dma = [[0, 0, 0]]
        dma_count = [0, 0, 0]
        for rec in self.ops:
            if rec[0] == "BAR":
                stages.append({e: [] for e in ENGINES})
                stage_dma.append(list(dma_count))
                continue
            if rec[0] == "DMA":
                _, ename, args, kw = rec
                which = 1 if ename == "gpsimd" else 0
                dma_count[which] += 16
                stages[-1][ename].append(("DMA", args, kw))
            elif rec[0] == "CC":
                _, ename, args, kw = rec
                dma_count[2] += 16
                stages[-1][ename].append(("CC", args, kw))
            else:
                ename, op, args, kw = rec
                stages[-1][ename].append((op, args, kw))
            stage_dma[-1] = list(dma_count)
        self.total_dma = list(dma_count)
        while len(stages) > 1 and not any(stages[-1].values()):
            stages.pop(); stage_dma.pop()
        nstage = len(stages)
        sb_target = [0] * (nstage + 1)
        for si in range(nstage):
            nact = sum(1 for e in ENGINES if stages[si][e])
            sb_target[si + 1] = sb_target[si] + 16 * nact

        nc = self.nc
        DS, DG, CCS, SB = self.DS, self.DG, self.CCS, self.SB
        block = self.ctx.enter_context(nc.Block())

        def make_body(ename):
            def body(eng):
                prev_wait = [0, 0, 0, 0]
                for si in range(nstage):
                    ops = stages[si][ename]
                    if not ops:
                        continue
                    if sb_target[si] > prev_wait[0]:
                        eng.wait_ge(SB, sb_target[si])
                        prev_wait[0] = sb_target[si]
                    d = stage_dma[si - 1] if si > 0 else [0, 0, 0]
                    if d[0] > prev_wait[1]:
                        eng.wait_ge(DS, d[0])
                        prev_wait[1] = d[0]
                    if d[1] > prev_wait[2]:
                        eng.wait_ge(DG, d[1])
                        prev_wait[2] = d[1]
                    if d[2] > prev_wait[3]:
                        eng.wait_ge(CCS, d[2])
                        prev_wait[3] = d[2]
                    last_c = None
                    for op, args, kw in ops:
                        try:
                            if op == "DMA":
                                sem = DG if ename == "gpsimd" else DS
                                eng.dma_start(out=args[0], in_=args[1], **kw).then_inc(sem, 16)
                            elif op == "CC":
                                kind, alu, groups, ins_, outs_ = args
                                eng.collective_compute(kind, alu, replica_groups=groups,
                                                       ins=ins_, outs=outs_).then_inc(CCS, 16)
                            else:
                                last_c = getattr(eng, op)(*args, **kw)
                        except Exception as ex:
                            raise RuntimeError(f"emit failed: {ename}.{op} args={[str(a)[:120] for a in args]}") from ex
                    if last_c is None:
                        eng.nop().then_inc(SB, 16)
                    else:
                        last_c.then_inc(SB, 16)
                eng.wait_ge(SB, sb_target[nstage])
                if dma_count[0] > 0:
                    eng.wait_ge(DS, dma_count[0])
                if dma_count[1] > 0:
                    eng.wait_ge(DG, dma_count[1])
                if dma_count[2] > 0:
                    eng.wait_ge(CCS, dma_count[2])
            return body

        for ename in ENGINES:
            getattr(block, ename)(make_body(ename))



def tblocks(T, sz=512):
    return [slice(i, min(i + sz, T)) for i in range(0, T, sz)]


def bcast_row(p, dest, src_row, T, bc_ps, ones_row1, extra_mul=None):
    for sl in tblocks(T):
        p.bar()
        w = sl.stop - sl.start
        p.t.matmul(bc_ps[:, 0:w], ones_row1[0:1, :], src_row[0:1, sl], start=True, stop=True)
        p.bar()
        if extra_mul is None:
            p.v.tensor_copy(dest[:, sl], bc_ps[:, 0:w])
        else:
            p.v.tensor_tensor(dest[:, sl], bc_ps[:, 0:w], extra_mul[:, sl], ALU.mult)


def ln_full(p, x_tiles, T, ln, out_bf=None, gamma=None, beta=None, out_f32=None,
            eps=EPS, post_cb=None):
    nd = len(x_tiles) * 128
    sx, sx2 = ln["stat_x"], ln["stat_x2"]
    for it, xt in enumerate(x_tiles):
        p.s.activation(ln["sq"][:, 0:T], xt, AF.Square)
        p.bar()
        first, last = it == 0, it == len(x_tiles) - 1
        for sl in tblocks(T):
            p.t.matmul(sx[0:1, sl], ln["ones_col"][:, 0:1], xt[:, sl], start=first, stop=last)
            p.t.matmul(sx2[0:1, sl], ln["ones_col"][:, 0:1], ln["sq"][:, sl], start=first, stop=last)
        p.bar()
    p.v.tensor_scalar(ln["mean"][0:1, 0:T], sx[0:1, 0:T], 1.0 / nd, None, ALU.mult)
    p.v.tensor_scalar(ln["ex2"][0:1, 0:T], sx2[0:1, 0:T], 1.0 / nd, None, ALU.mult)
    p.bar()
    p.v.tensor_tensor(ln["c1"][0:1, 0:T], ln["mean"][0:1, 0:T], ln["mean"][0:1, 0:T], ALU.mult)
    p.bar()
    p.v.tensor_tensor(ln["ex2"][0:1, 0:T], ln["ex2"][0:1, 0:T], ln["c1"][0:1, 0:T], ALU.subtract)
    p.bar()
    p.s.activation(ln["c1"][0:1, 0:T], ln["ex2"][0:1, 0:T], AF.Ln, bias=p.cf(eps))
    p.bar()
    p.s.activation(ln["c1"][0:1, 0:T], ln["c1"][0:1, 0:T], AF.Exp, scale=p.cf(-0.5))
    p.bar()
    p.v.tensor_tensor(ln["c2"][0:1, 0:T], ln["mean"][0:1, 0:T], ln["c1"][0:1, 0:T], ALU.mult)
    p.bar()
    p.v.tensor_scalar(ln["c2"][0:1, 0:T], ln["c2"][0:1, 0:T], -1.0, None, ALU.mult)
    bcast_row(p, ln["bc1"], ln["c1"][0:1, 0:T], T, ln["bc_ps"], ln["ones_row1"])
    bcast_row(p, ln["bc2"], ln["c2"][0:1, 0:T], T, ln["bc_ps"], ln["ones_row1"])
    p.bar()
    for it, xt in enumerate(x_tiles):
        p.v.tensor_tensor(ln["sq"][:, 0:T], xt, ln["bc1"][:, 0:T], ALU.mult)
        p.bar()
        p.v.tensor_tensor(ln["sq"][:, 0:T], ln["sq"][:, 0:T], ln["bc2"][:, 0:T], ALU.add)
        p.bar()
        src = ln["sq"][:, 0:T]
        if gamma is not None:
            p.v.tensor_scalar(src, src, gamma[it], beta[it], ALU.mult, ALU.add)
            p.bar()
        if out_bf is not None and out_bf[it] is not None:
            p.v.tensor_copy(out_bf[it], src)
        if out_f32 is not None and out_f32[it] is not None:
            p.s.copy(out_f32[it], src)
        if post_cb is not None:
            post_cb(it, src)
        p.bar()


def make_ln(al, T, tag=""):
    ln = {}
    ln["sq"] = al.sb("ln_sq" + tag, (128, T), F32)
    for r in ("mean", "ex2", "c1"):
        ln[r] = al.sb("ln_" + r + tag, (1, T), F32)
    ln["c2"] = ln["ex2"]
    ln["bc1"] = al.sb("ln_bc1" + tag, (128, T), F32)
    ln["bc2"] = al.sb("ln_bc2" + tag, (128, T), F32)
    ln["ones_col"] = al.sb("ln_onesc" + tag, (128, 1), F32)
    ln["ones_row1"] = al.sb("ln_onesr" + tag, (1, 128), F32)
    return ln


def init_consts(p, ln):
    p.v.memset(ln["ones_col"][:], 1.0)
    p.v.memset(ln["ones_row1"][:], 1.0)


def stream_proj(p, wdram_blocks, nin, nout, x_tiles, T, wbufs, mm_ps, evac_fn,
                dma_fn=None, accum_extra=None):
    nko = nin // 128
    noc = nout // 128
    tb = tblocks(T)
    for oc in range(noc):
        wb = wbufs[oc % 2]
        p.dma(wb[:].rearrange("p (c m) -> p c m", m=128),
              wdram_blocks[oc].rearrange("(c p) m -> p c m", p=128), engine="gpsimd")
        p.bar()
        for ib, sl in enumerate(tb):
            w = sl.stop - sl.start
            for kc in range(nko):
                p.t.matmul(mm_ps[:, ib * 512:ib * 512 + w],
                           wb[:, kc * 128:(kc + 1) * 128],
                           x_tiles[kc][:, sl],
                           start=(kc == 0), stop=(kc == nko - 1 and accum_extra is None))
            if accum_extra is not None:
                accum_extra(oc, ib, sl)
        p.bar()
        for ib, sl in enumerate(tb):
            evac_fn(oc, ib, sl, mm_ps[:, ib * 512:ib * 512 + (sl.stop - sl.start)])
        p.bar()
        if dma_fn is not None:
            for ib, sl in enumerate(tb):
                dma_fn(oc, ib, sl)
            p.bar()



def build_LA(S=4096, fused=None):
    T = S // 2
    if fused is None:
        nc = bass.Bass("TRN2", target_bir_lowering=False)
        ctx = ExitStack()
        p = Prog(nc, ctx)
        dov, pfx = {}, ""
    else:
        p, dov, pfx = fused
        nc, ctx = p.nc, None

    def _dt(name, shape, dtype, kind="ExternalInput"):
        if name in dov:
            return dov[name]
        return nc.dram_tensor(pfx + name, shape, dtype, kind=kind).ap()
    sc = p.sb_scope()
    x0 = _dt("x0", [8, 128, T], F32, kind="ExternalInput")
    wq = _dt("wq", [8, 1024, 128], F32, kind="ExternalInput")
    wv = _dt("wv", [8, 1024, 128], F32, kind="ExternalInput")
    wg = _dt("wg", [8, 1024, 128], F32, kind="ExternalInput")
    o_xn = _dt("o_xn", [8, 128, T], BF16, kind="ExternalOutput")
    o_q = _dt("o_q", [8, 128, T], F32, kind="ExternalOutput")
    o_v = _dt("o_v", [8, 128, T], F32, kind="ExternalOutput")
    o_gs = _dt("o_gs", [8, 128, T], F32, kind="ExternalOutput")

    xt = [sc.sb(f"sx{i}", (128, T), F32) for i in range(8)]
    xnb = [sc.sb(f"xn{i}", (128, T), BF16) for i in range(8)]
    wbufs = [sc.sb(f"wb{i}", (128, 1024), BF16) for i in range(2)]
    ot = sc.sb("ot", (128, T), F32)
    ln = make_ln(sc, T)
    psA, psB = p.psA, p.psB

    init_consts(p, ln)
    for i in range(8):
        p.dma(xt[i][:], x0[i])
    p.bar()
    ln_full(p, [t[:] for t in xt], T,
            dict(ln, stat_x=psA[0:1, 0:T], stat_x2=psB[0:1, 0:T], bc_ps=psB[:, 0:512]),
            out_bf=[t[:] for t in xnb])

    def evac_copy(oc, ib, sl, ps):
        p.v.tensor_copy(ot[:, sl], ps)

    def evac_silu(oc, ib, sl, ps):
        p.s.activation(ot[:, sl], ps, AF.Silu)

    def dma_to(out_dram):
        def f(oc, ib, sl):
            p.dma(out_dram[oc][:, sl], ot[:, sl])
        return f

    xnt = [t[:] for t in xnb]
    stream_proj(p, wq, 1024, 1024, xnt, T, wbufs, psA, evac_copy, dma_to(o_q))
    stream_proj(p, wv, 1024, 1024, xnt, T, wbufs, psA, evac_copy, dma_to(o_v))
    stream_proj(p, wg, 1024, 1024, xnt, T, wbufs, psA, evac_silu, dma_to(o_gs))
    for i in range(8):
        p.dma(o_xn[i], xnb[i][:])
    sc.close()
    if fused is None:
        p.finalize()
        return nc, ctx


def host_LA(inputs, x0_dmaj, S):
    wq = prep_blocks(f32c(inputs["L0_attn_Wq"].T))
    wv = prep_blocks(f32c(inputs["L0_attn_Wv"].T))
    wg = prep_blocks(f32c(inputs["L0_attn_Wg"].T))
    T = S // 2
    maps = []
    for c in range(NCORES):
        b, h = c // 2, c % 2
        xs = np.ascontiguousarray(x0_dmaj[b, :, h * T:(h + 1) * T].reshape(8, 128, T))
        maps.append(dict(x0=xs, wq=wq, wv=wv, wg=wg))
    return maps


def build_LB(S=4096, fused=None):
    T = S // 2
    if fused is None:
        nc = bass.Bass("TRN2", target_bir_lowering=False)
        ctx = ExitStack()
        p = Prog(nc, ctx)
        dov, pfx = {}, ""
    else:
        p, dov, pfx = fused
        nc, ctx = p.nc, None

    def _dt(name, shape, dtype, kind="ExternalInput"):
        if name in dov:
            return dov[name]
        return nc.dram_tensor(pfx + name, shape, dtype, kind=kind).ap()
    sc = p.sb_scope()
    fo = dov.get("_fo", 0)
    vd = _dt("vd", [2, 4, 128, T], F32)
    qd = _dt("qd", [2, 4, 128, T], F32)
    lam = _dt("lam", [128, 4], F32)
    o_y = _dt("o_y", [2, 4, 128, T], F32, kind="ExternalOutput")

    vt = sc.sb("vt", (128, S), F32)
    qt = sc.sb("qt", (128, S), F32)
    lamc = sc.sb("lamc", (128, 4), F32)
    ones = sc.sb("ones", (128, S), F32)
    lamt = sc.sb("lamt", (128, S), F32)
    stt = sc.sb("stt", (128, S), F32)

    p.dma(lamc[:, 0:4], lam)
    p.v.memset(ones[:], 1.0)
    p.bar()
    for i in range(4):
        for j in range(2):
            p.dma(vt[:, j * T:(j + 1) * T], vd[j, fo + i])
            p.dma(qt[:, j * T:(j + 1) * T], qd[j, fo + i])
        p.s.activation(lamt[:], ones[:], AF.Copy, scale=lamc[:, i:i + 1])
        p.bar()
        p.v.tensor_tensor_scan(stt[:], lamt[:], vt[:], 0.0, ALU.mult, ALU.add)
        p.bar()
        p.v.tensor_tensor(stt[:], stt[:], qt[:], ALU.mult)
        p.bar()
        for j in range(2):
            p.dma(o_y[j, i], stt[:, j * T:(j + 1) * T])
        p.bar()
    sc.close()
    if fused is None:
        p.finalize()
        return nc, ctx


def host_LB(inputs, q_full, v_full, S):
    T = S // 2
    lam = np.repeat(f32c(inputs["L0_attn_decay"])[:, 1, 0], 64)
    maps = []
    for c in range(NCORES):
        b, g = c // 2, c % 2
        chs = slice(g * 512, (g + 1) * 512)

        def spl(x):
            a = x[b, chs].reshape(4, 128, S)
            return np.ascontiguousarray(np.stack([a[:, :, 0:T], a[:, :, T:S]]))
        maps.append(dict(vd=spl(v_full), qd=spl(q_full),
                         lam=np.ascontiguousarray(lam[chs].reshape(4, 128).T)))
    return maps


def build_LC1(S=4096, fused=None):
    T = S // 2
    if fused is None:
        nc = bass.Bass("TRN2", target_bir_lowering=False)
        ctx = ExitStack()
        p = Prog(nc, ctx)
        dov, pfx = {}, ""
    else:
        p, dov, pfx = fused
        nc, ctx = p.nc, None

    def _dt(name, shape, dtype, kind="ExternalInput"):
        if name in dov:
            return dov[name]
        return nc.dram_tensor(pfx + name, shape, dtype, kind=kind).ap()
    sc = p.sb_scope()
    fo = dov.get("_fo", 0)
    yv = _dt("yv", [2, 4, 128, T], F32, kind="ExternalInput")
    gsv = _dt("gsv", [8, 128, T], F32, kind="ExternalInput")
    x0 = _dt("x0", [8, 128, T], F32, kind="ExternalInput")
    lgb = _dt("lgb", [128, 16], F32, kind="ExternalInput")
    wo = _dt("wo", [8, 1024, 128], F32, kind="ExternalInput")
    o_xp = _dt("o_xp", [8, 128, T], F32, kind="ExternalOutput")

    yt = [sc.sb(f"sy{i}", (128, T), F32) for i in range(8)]
    ygb = [sc.sb(f"yg{i}", (128, T), BF16) for i in range(8)]
    gst = sc.sb("gst", (128, T), F32)
    lgbt = sc.sb("lgbt", (128, 16), F32)
    xot = sc.sb("xot", (128, T), F32)
    wbufs = [sc.sb(f"wb{i}", (128, 1024), BF16) for i in range(2)]
    ln = make_ln(sc, T)
    psA, psB = p.psA, p.psB

    init_consts(p, ln)
    for i in range(8):
        p.dma(yt[i][:], yv[i // 4, fo + i % 4])
    p.dma(lgbt[:, 0:16], lgb)
    p.bar()

    ln_full(p, [t[:] for t in yt], T,
            dict(ln, stat_x=psA[0:1, 0:T], stat_x2=psB[0:1, 0:T], bc_ps=psB[:, 0:512]),
            gamma=[lgbt[:, 2 * i:2 * i + 1] for i in range(8)],
            beta=[lgbt[:, 2 * i + 1:2 * i + 2] for i in range(8)],
            out_f32=[t[:] for t in yt])
    for i in range(8):
        p.dma(gst[:, 0:T], gsv[i])
        p.bar()
        p.v.tensor_tensor(ygb[i][:], yt[i][:], gst[:, 0:T], ALU.mult)
        p.bar()

    def evac(oc, ib, sl, ps):
        p.v.tensor_tensor(xot[:, sl], ps, yt[oc][:, sl], ALU.add)

    def dmaf(oc, ib, sl):
        p.dma(o_xp[oc][:, sl], xot[:, sl])

    for i in range(8):
        p.dma(yt[i][:], x0[i])
    p.bar()
    stream_proj(p, wo, 1024, 1024, [t[:] for t in ygb], T, wbufs, psA, evac, dmaf)
    sc.close()
    if fused is None:
        p.finalize()
        return nc, ctx


def host_LC1(inputs, y_dmaj, gs_by_core, x0_dmaj, S):
    T = S // 2
    wo = prep_blocks(f32c(inputs["L0_attn_Wo"].T))
    g = f32c(inputs["L0_attn_ln_g"]).reshape(8, 128)
    b = f32c(inputs["L0_attn_ln_b"]).reshape(8, 128)
    lgb = np.ascontiguousarray(np.stack([g, b], axis=2).transpose(1, 0, 2).reshape(128, 16))
    maps = []
    for c in range(NCORES):
        bb, h = c // 2, c % 2
        sl = slice(h * T, (h + 1) * T)
        maps.append(dict(
            yv=np.ascontiguousarray(y_dmaj[bb, :, sl].reshape(2, 4, 128, T)),
            gsv=gs_by_core[c].reshape(8, 128, T),
            x0=np.ascontiguousarray(x0_dmaj[bb, :, sl].reshape(8, 128, T)),
            lgb=lgb, wo=wo))
    return maps


def build_LC2(S=4096, fused=None):
    T = S // 2
    TB = min(512, T)
    ntb = T // TB
    if fused is None:
        nc = bass.Bass("TRN2", target_bir_lowering=False)
        ctx = ExitStack()
        p = Prog(nc, ctx)
        dov, pfx = {}, ""
    else:
        p, dov, pfx = fused
        nc, ctx = p.nc, None

    def _dt(name, shape, dtype, kind="ExternalInput"):
        if name in dov:
            return dov[name]
        return nc.dram_tensor(pfx + name, shape, dtype, kind=kind).ap()
    sc = p.sb_scope()
    xnd = _dt("xnd", [8, 128, T], BF16, kind="ExternalInput")
    xpd = _dt("xpd", [8, 128, T], F32, kind="ExternalInput")
    rw = _dt("rw", [1024, 8], F32, kind="ExternalInput")
    rb = _dt("rb", [8, 1], F32, kind="ExternalInput")
    w1b = _dt("w1b", [8, 16, 1024, 128], F32, kind="ExternalInput")
    b1c = _dt("b1c", [128, 128], F32, kind="ExternalInput")
    w2b = _dt("w2b", [8, 8, 2048, 128], F32, kind="ExternalInput")
    b2r = _dt("b2r", [8, 1024], F32, kind="ExternalInput")
    sw1b = _dt("sw1b", [2, 16, 1024, 128], F32, kind="ExternalInput")
    sb1c = _dt("sb1c", [128, 32], F32, kind="ExternalInput")
    sw2b = _dt("sw2b", [2, 8, 2048, 128], F32, kind="ExternalInput")
    sb2c = _dt("sb2c", [128, 16], F32, kind="ExternalInput")
    iden = _dt("iden", [128, 128], F32, kind="ExternalInput")
    indd = _dt("indd", [8, 1024], F32, kind="ExternalInput")
    o_x1 = _dt("o_x1", [8, 128, T], F32, kind="ExternalOutput")
    o_xn1 = _dt("o_xn1", [8, 128, T], BF16, kind="ExternalOutput")
    o_pp = _dt("o_pp", [128, 8], F32, kind="ExternalOutput")

    xnt = [sc.sb(f"sxn{i}", (128, T), BF16) for i in range(8)]
    xpt = [sc.sb(f"sxp{i}", (128, T), F32) for i in range(8)]
    rwt = sc.sb("rwt", (128, 64), BF16)
    rbt = sc.sb("rbt", (8, 1), F32)
    b1t = sc.sb("b1t", (128, 128), F32)
    b2t = sc.sb("b2t", (8, 1024), BF16)
    sb1t = sc.sb("sb1t", (128, 32), F32)
    sb2t = sc.sb("sb2t", (128, 16), F32)
    sbht = sc.sb("sbht", (128, 8), F32)
    idt = sc.sb("idt", (128, 128), F32)
    ones_bf = sc.sb("ones_bf", (1, 128), BF16)
    indbf = sc.sb("indbf", (8, 1024), BF16)
    logit = sc.sb("logit", (8, T), F32)
    w8T = sc.sb("w8T", (8, T), BF16)
    tm = sc.sb("tm", (128, 8), F32)
    tm2 = sc.sb("tm2", (128, 8), F32)
    m1 = sc.sb("m1", (128, 1), F32)
    nm1 = sc.sb("nm1", (128, 1), F32)
    m2 = sc.sb("m2", (128, 1), F32)
    dcol = sc.sb("dcol", (128, 1), F32)
    ppc = sc.sb("ppc", (128, 8), F32)
    ln = make_ln(sc, T)
    psA, psB = p.psA, p.psB
    _sc = p.sb_scope()
    het = [_sc.sb(f"he{i}", (128, TB), BF16) for i in range(16)]
    wbc = _sc.sb("wbc", (128, TB), F32)
    tmp = _sc.sb("tmpe", (128, TB), F32)
    wbufs = [_sc.sb(f"wb{i}", (128, 1024), BF16) for i in range(2)]
    wbufs2 = [_sc.sb(f"wc{i}", (128, 2048), BF16) for i in range(2)]

    init_consts(p, ln)
    p.v.memset(ones_bf[:], 1.0)
    for i in range(8):
        p.dma(xnt[i][:], xnd[i])
        p.dma(xpt[i][:], xpd[i])
    p.dma(rwt[:].rearrange("p (c e) -> p c e", e=8), rw.rearrange("(c p) e -> p c e", p=128), engine="gpsimd")
    p.dma(rbt[:], rb)
    p.dma(b1t[:], b1c)
    p.dma(b2t[:], b2r, engine="gpsimd")
    p.dma(sb1t[:], sb1c)
    p.dma(sb2t[:], sb2c)
    p.dma(idt[:], iden)
    p.dma(indbf[:], indd, engine="gpsimd")
    p.bar()
    p.v.tensor_tensor(sbht[:, 0:8], sb2t[:].rearrange("p (c s) -> p c s", s=2)[:, :, 0],
                      sb2t[:].rearrange("p (c s) -> p c s", s=2)[:, :, 1], ALU.add)
    p.bar()
    p.v.tensor_scalar(sbht[:, 0:8], sbht[:, 0:8], 0.5, None, ALU.mult)
    for sl in tblocks(T):
        for kc in range(8):
            p.t.matmul(psB[0:8, sl], rwt[:, kc * 8:(kc + 1) * 8], xnt[kc][:, sl],
                       start=(kc == 0), stop=(kc == 7))
    p.bar()
    p.v.tensor_scalar(logit[0:8, 0:T], psB[0:8, 0:T], rbt[0:8, 0:1], None, ALU.add)
    p.bar()
    nch = T // 128
    for chv in range(nch):
        clsl = slice(chv * 128, (chv + 1) * 128)
        p.t.transpose(psA[:, 0:8][0:128, 0:8], logit[0:8, clsl], idt[0:8, 0:8])
        p.bar()
        p.v.tensor_copy(tm[:, 0:8], psA[0:128, 0:8])
        p.bar()
        p.v.reduce_max(m1[:, 0:1], tm[:, 0:8], AX.X)
        p.v.reduce_max(nm1[:, 0:1], tm[:, 0:8], AX.X, negate=True)
        p.bar()
        p.v.tensor_scalar(tm2[:, 0:8], tm[:, 0:8], m1[:, 0:1], None, ALU.is_ge)
        p.bar()
        p.v.tensor_scalar(tm2[:, 0:8], tm2[:, 0:8], -1e30, None, ALU.mult)
        p.bar()
        p.v.tensor_tensor(tm2[:, 0:8], tm2[:, 0:8], tm[:, 0:8], ALU.add)
        p.bar()
        p.v.reduce_max(m2[:, 0:1], tm2[:, 0:8], AX.X)
        p.bar()
        p.v.tensor_scalar(tm2[:, 0:8], tm[:, 0:8], m2[:, 0:1], None, ALU.is_ge)
        p.v.tensor_tensor(dcol[:, 0:1], m2[:, 0:1], nm1[:, 0:1], ALU.add)
        p.bar()
        p.s.activation(dcol[:, 0:1], dcol[:, 0:1], AF.Exp)
        p.s.activation(tm[:, 0:8], tm[:, 0:8], AF.Exp, bias=nm1[:, 0:1])
        p.bar()
        p.v.tensor_scalar(dcol[:, 0:1], dcol[:, 0:1], 1.0, None, ALU.add)
        p.bar()
        p.v.reciprocal(dcol[:, 0:1], dcol[:, 0:1])
        p.bar()
        p.v.tensor_tensor(tm[:, 0:8], tm[:, 0:8], tm2[:, 0:8], ALU.mult)
        p.bar()
        p.v.tensor_scalar(tm[:, 0:8], tm[:, 0:8], dcol[:, 0:1], None, ALU.mult)
        p.bar()
        p.t.transpose(psA[0:8, 0:128], tm[:, 0:8], idt[:])
        p.bar()
        p.v.tensor_copy(w8T[0:8, clsl], psA[0:8, 0:128])
        p.bar()

    for e in range(8):
        for tb in range(ntb):
            toff = tb * TB
            for oc in range(16):
                wb = wbufs[oc % 2]
                p.dma(wb[:].rearrange("p (c m) -> p c m", m=128),
                      w1b[e, oc].rearrange("(c p) m -> p c m", p=128), engine="gpsimd")
                p.bar()
                for ib, sl in enumerate(tblocks(TB)):
                    gsl = slice(toff + sl.start, toff + sl.stop)
                    w = sl.stop - sl.start
                    for kc in range(8):
                        p.t.matmul(psA[:, ib * 512:ib * 512 + w], wb[:, kc * 128:(kc + 1) * 128],
                                   xnt[kc][:, gsl], start=(kc == 0), stop=(kc == 7))
                p.bar()
                for ib, sl in enumerate(tblocks(TB)):
                    w = sl.stop - sl.start
                    p.s.activation(het[oc][:, sl], psA[:, ib * 512:ib * 512 + w], AF.Gelu,
                                   bias=b1t[:, e * 16 + oc:e * 16 + oc + 1])
                p.bar()
            for ib, sl in enumerate(tblocks(TB)):
                gsl = slice(toff + sl.start, toff + sl.stop)
                w = sl.stop - sl.start
                p.t.matmul(psB[:, 0:w], indbf[0:8, e * 128:(e + 1) * 128], w8T[0:8, gsl],
                           start=True, stop=True)
                p.bar()
                p.v.tensor_copy(wbc[:, sl], psB[:, 0:w])
                p.bar()
            for oc in range(8):
                wb = wbufs2[oc % 2]
                p.dma(wb[:].rearrange("p (c m) -> p c m", m=128),
                      w2b[e, oc].rearrange("(c p) m -> p c m", p=128), engine="gpsimd")
                p.bar()
                for ib, sl in enumerate(tblocks(TB)):
                    gsl = slice(toff + sl.start, toff + sl.stop)
                    w = sl.stop - sl.start
                    for kc in range(16):
                        p.t.matmul(psA[:, ib * 512:ib * 512 + w], wb[:, kc * 128:(kc + 1) * 128],
                                   het[kc][:, sl], start=(kc == 0), stop=False)
                    p.t.matmul(psA[:, ib * 512:ib * 512 + w], b2t[0:8, oc * 128:(oc + 1) * 128],
                               w8T[0:8, gsl], start=False, stop=True)
                p.bar()
                for ib, sl in enumerate(tblocks(TB)):
                    w = sl.stop - sl.start
                    p.v.tensor_tensor(tmp[:, sl], psA[:, ib * 512:ib * 512 + w], wbc[:, sl], ALU.mult)
                p.bar()
                for ib, sl in enumerate(tblocks(TB)):
                    gsl = slice(toff + sl.start, toff + sl.stop)
                    p.v.tensor_tensor(xpt[oc][:, gsl], xpt[oc][:, gsl], tmp[:, sl], ALU.add)
                p.bar()
    for tb in range(ntb):
        toff = tb * TB
        for si in range(2):
            for oc in range(16):
                wb = wbufs[oc % 2]
                p.dma(wb[:].rearrange("p (c m) -> p c m", m=128),
                      sw1b[si, oc].rearrange("(c p) m -> p c m", p=128), engine="gpsimd")
                p.bar()
                for ib, sl in enumerate(tblocks(TB)):
                    gsl = slice(toff + sl.start, toff + sl.stop)
                    w = sl.stop - sl.start
                    for kc in range(8):
                        p.t.matmul(psB[:, ib * 512:ib * 512 + w], wb[:, kc * 128:(kc + 1) * 128],
                                   xnt[kc][:, gsl], start=(kc == 0), stop=(kc == 7))
                p.bar()
                for ib, sl in enumerate(tblocks(TB)):
                    w = sl.stop - sl.start
                    p.s.activation(het[oc][:, sl], psB[:, ib * 512:ib * 512 + w], AF.Gelu,
                                   bias=sb1t[:, si * 16 + oc:si * 16 + oc + 1])
                p.bar()
            for oc in range(8):
                wb = wbufs2[oc % 2]
                p.dma(wb[:].rearrange("p (c m) -> p c m", m=128),
                      sw2b[si, oc].rearrange("(c p) m -> p c m", p=128), engine="gpsimd")
                p.bar()
                for ib, sl in enumerate(tblocks(TB)):
                    w = sl.stop - sl.start
                    for kc in range(16):
                        p.t.matmul(psA[:, ib * 512:ib * 512 + w], wb[:, kc * 128:(kc + 1) * 128],
                                   het[kc][:, sl], start=(kc == 0), stop=(kc == 15))
                p.bar()
                for ib, sl in enumerate(tblocks(TB)):
                    gsl = slice(toff + sl.start, toff + sl.stop)
                    w = sl.stop - sl.start
                    if si == 0:
                        p.v.tensor_scalar(tmp[:, sl], psA[:, ib * 512:ib * 512 + w], 0.5, None, ALU.mult)
                    else:
                        p.v.tensor_scalar(tmp[:, sl], psA[:, ib * 512:ib * 512 + w], 0.5,
                                          sbht[:, oc:oc + 1], ALU.mult, ALU.add)
                    p.bar()
                    p.v.tensor_tensor(xpt[oc][:, gsl], xpt[oc][:, gsl], tmp[:, sl], ALU.add)
                p.bar()
    for i in range(8):
        p.dma(o_x1[i], xpt[i][:])
    p.bar()
    _sc.close()

    def post(it, src):
        p.v.reduce_sum(ppc[:, it:it + 1], src, AX.X)

    ln_full(p, [t[:] for t in xpt], T,
            dict(ln, stat_x=psA[0:1, 0:T], stat_x2=psB[0:1, 0:T], bc_ps=psB[:, 1024:1536]),
            out_bf=[t[:] for t in xnt], post_cb=post)
    for i in range(8):
        p.dma(o_xn1[i], xnt[i][:])
    p.dma(o_pp[:], ppc[:])
    sc.close()
    if fused is None:
        p.finalize()
        return nc, ctx


def host_LC2(inputs, xn_by_core, xp_by_core, S):
    E, Dm, H = 8, 1024, 2048
    w1 = f32c(inputs["L0_ffn_w1"])
    w2 = f32c(inputs["L0_ffn_w2"])
    sw1 = f32c(inputs["L0_ffn_sw1"])
    sw2 = f32c(inputs["L0_ffn_sw2"])
    w1b = np.stack([prep_blocks(w1[e]) for e in range(E)])
    w2b = np.stack([prep_blocks(w2[e]) for e in range(E)])
    sw1b = np.stack([prep_blocks(sw1[s]) for s in range(2)])
    sw2b = np.stack([prep_blocks(sw2[s]) for s in range(2)])
    b1 = f32c(inputs["L0_ffn_b1"])
    b1c = np.ascontiguousarray(b1.reshape(E, 16, 128).transpose(2, 0, 1).reshape(128, 128))
    b2r = f32c(inputs["L0_ffn_b2"])
    sb1 = f32c(inputs["L0_ffn_sb1"])
    sb1c = np.ascontiguousarray(sb1.reshape(2, 16, 128).transpose(2, 0, 1).reshape(128, 32))
    sb2 = f32c(inputs["L0_ffn_sb2"])
    sb2c = np.ascontiguousarray(sb2.reshape(2, 8, 128).transpose(2, 1, 0).reshape(128, 16))
    rw = f32c(inputs["L0_ffn_router_w"])
    rb = f32c(inputs["L0_ffn_router_b"]).reshape(8, 1)
    iden = np.eye(128, dtype=np.float32)
    ind = np.zeros((8, 1024), np.float32)
    for e in range(8):
        ind[e, e * 128:(e + 1) * 128] = 1.0
    maps = []
    for c in range(NCORES):
        maps.append(dict(xnd=xn_by_core[c], xpd=xp_by_core[c], rw=rw, rb=rb,
                         w1b=w1b, b1c=b1c, w2b=w2b, b2r=b2r, sw1b=sw1b, sb1c=sb1c,
                         sw2b=sw2b, sb2c=sb2c, iden=iden, indd=ind))
    return maps


def emit_svd_moe(p, x_tiles, T, wm_blocks, vtd, utd, scol, bcol_ap, out_cb,
                 wbufs, vbuf, ubuf, xvs_tiles, psA, psB, scale=1.0, nin=1024,
                 dma_cb=None):
    nko = nin // 128
    for e in range(8):
        p.dma(vbuf[:].rearrange("p (c r) -> p c r", r=16),
              vtd[e].rearrange("(c p) r -> p c r", p=128), engine="gpsimd")
        p.bar()
        for ib, sl in enumerate(tblocks(T)):
            w = sl.stop - sl.start
            for kc in range(nko):
                p.t.matmul(psB[0:16, sl], vbuf[:, kc * 16:(kc + 1) * 16],
                           x_tiles[kc][:, sl], start=(kc == 0), stop=(kc == nko - 1))
        p.bar()
        p.v.tensor_scalar(xvs_tiles[e][0:16, 0:T], psB[0:16, 0:T], scol[0:16, e:e + 1],
                          None, ALU.mult)
        p.bar()
    for oc in range(8):
        wb = wbufs[oc % 2]
        p.dma(wb[:].rearrange("p (c m) -> p c m", m=128),
              wm_blocks[oc].rearrange("(c p) m -> p c m", p=128), engine="gpsimd")
        for e in range(8):
            p.dma(ubuf[0:16, e * 128:(e + 1) * 128], utd[e, :, oc * 128:(oc + 1) * 128],
                  engine="gpsimd")
        p.bar()
        for ib, sl in enumerate(tblocks(T)):
            w = sl.stop - sl.start
            for kc in range(nko):
                p.t.matmul(psA[:, ib * 512:ib * 512 + w], wb[:, kc * 128:(kc + 1) * 128],
                           x_tiles[kc][:, sl], start=(kc == 0), stop=False)
            for e in range(8):
                p.t.matmul(psA[:, ib * 512:ib * 512 + w], ubuf[0:16, e * 128:(e + 1) * 128],
                           xvs_tiles[e][0:16, sl], start=False, stop=(e == 7))
        p.bar()
        for ib, sl in enumerate(tblocks(T)):
            w = sl.stop - sl.start
            out_cb(oc, ib, sl, psA[:, ib * 512:ib * 512 + w])
        p.bar()
        if dma_cb is not None:
            for ib, sl in enumerate(tblocks(T)):
                dma_cb(oc, ib, sl)
            p.bar()


def build_LD(S=4096, fused=None):
    T = S // 2
    if fused is None:
        nc = bass.Bass("TRN2", target_bir_lowering=False)
        ctx = ExitStack()
        p = Prog(nc, ctx)
        dov, pfx = {}, ""
    else:
        p, dov, pfx = fused
        nc, ctx = p.nc, None

    def _dt(name, shape, dtype, kind="ExternalInput"):
        if name in dov:
            return dov[name]
        return nc.dram_tensor(pfx + name, shape, dtype, kind=kind).ap()
    sc = p.sb_scope()
    xnd = _dt("xnd", [8, 128, T], BF16, kind="ExternalInput")
    ppd = _dt("ppd", [8, 128, 8], F32, kind="ExternalInput")
    gw1 = _dt("gw1", [2, 1024, 128], F32, kind="ExternalInput")
    gb1 = _dt("gb1", [128, 2], F32, kind="ExternalInput")
    gw2 = _dt("gw2", [2, 128, 8], F32, kind="ExternalInput")
    gb2 = _dt("gb2", [8, 1], F32, kind="ExternalInput")
    win = _dt("win", [24, 1024, 128], F32, kind="ExternalInput")
    wmq = _dt("wmq", [8, 1024, 128], F32, kind="ExternalInput")
    wmk = _dt("wmk", [8, 1024, 128], F32, kind="ExternalInput")
    wmv = _dt("wmv", [8, 1024, 128], F32, kind="ExternalInput")
    vtq = _dt("vtq", [8, 1024, 16], F32, kind="ExternalInput")
    vtk = _dt("vtk", [8, 1024, 16], F32, kind="ExternalInput")
    vtv = _dt("vtv", [8, 1024, 16], F32, kind="ExternalInput")
    utq = _dt("utq", [8, 16, 1024], F32, kind="ExternalInput")
    utk = _dt("utk", [8, 16, 1024], F32, kind="ExternalInput")
    utv = _dt("utv", [8, 16, 1024], F32, kind="ExternalInput")
    smat = _dt("smat", [16, 24], F32, kind="ExternalInput")
    bq = _dt("bq", [128, 24], F32, kind="ExternalInput")
    iden = _dt("iden", [128, 128], F32, kind="ExternalInput")
    bsel = _dt("bsel", [4, 1], F32, kind="ExternalInput")

    o_k1 = _dt("o_k1", [8, 128, T], BF16, kind="ExternalOutput")
    o_v1 = _dt("o_v1", [8, 128, T], BF16, kind="ExternalOutput")
    o_q1 = _dt("o_q1", [8, 128, T], BF16, kind="ExternalOutput")
    o_qf = _dt("o_qf", [8, 128, T], BF16, kind="ExternalOutput")
    o_kf = _dt("o_kf", [8, 128, T], BF16, kind="ExternalOutput")
    o_vf = _dt("o_vf", [8, 128, T], BF16, kind="ExternalOutput")
    o_w8 = _dt("o_w8", [4, 8], F32, kind="ExternalOutput")

    xnt = [sc.sb(f"sxn{i}", (128, T), BF16) for i in range(8)]
    ppt = sc.sb("ppt", (128, 64), F32)
    pool = sc.sb("pool", (128, 32), BF16)
    gh = sc.sb("gh", (128, 8), BF16)
    gb1t = sc.sb("gb1t", (128, 2), F32)
    gb2t = sc.sb("gb2t", (8, 1), F32)
    logit = sc.sb("lgt", (8, 4), F32)
    idt = sc.sb("idt", (128, 128), F32)
    tm = sc.sb("tm", (4, 8), F32)
    tm2 = sc.sb("tm2", (4, 8), F32)
    m1 = sc.sb("m1", (4, 1), F32)
    nm1 = sc.sb("nm1", (4, 1), F32)
    m2 = sc.sb("m2", (4, 1), F32)
    dcol = sc.sb("dcol", (4, 1), F32)
    tmbf = sc.sb("tmbf", (4, 8), BF16)
    smt = sc.sb("smt", (16, 24), F32)
    scol = sc.sb("scol", (16, 8), F32)
    bqt = sc.sb("bqt", (128, 24), F32)
    ot = sc.sb("ot", (128, T), BF16)
    wbufs = [sc.sb(f"wb{i}", (128, 1024), BF16) for i in range(2)]
    vbuf = sc.sb("vbuf", (128, 128), BF16)
    ubuf = sc.sb("ubuf", (16, 1024), BF16)
    xvst = [sc.sb(f"xvs{e}", (16, T), BF16) for e in range(8)]
    bselt = sc.sb("bselt", (4, 1), BF16)
    wselr = sc.sb("wselr", (1, 8), F32)
    wselb = sc.sb("wselb", (1, 8), BF16)
    wselc = sc.sb("wselc", (16, 8), F32)
    onesb = sc.sb("onesb", (1, 128), BF16)
    psA, psB = p.psA, p.psB

    for i in range(8):
        p.dma(xnt[i][:], xnd[i])
    p.dma(ppt[:].rearrange("p (c d) -> p c d", d=8), ppd.rearrange("c p d -> p c d"))
    p.dma(gb1t[:], gb1)
    p.dma(gb2t[:], gb2)
    p.dma(idt[:], iden)
    p.dma(smt[:], smat)
    p.dma(bqt[:], bq)
    p.dma(bselt[:], bsel, engine="gpsimd")
    p.v.memset(onesb[:], 1.0)
    p.bar()
    ppv = ppt[:].rearrange("p (c d) -> p c d", d=8)
    p.v.tensor_tensor(pool[:].rearrange("p (b d) -> p b d", d=8),
                      ppv[:, 0:8:2, :], ppv[:, 1:8:2, :], ALU.add)
    p.bar()
    p.v.tensor_scalar(pool[:, 0:32], pool[:, 0:32], 1.0 / S, None, ALU.mult)
    p.bar()
    for cc in range(2):
        p.dma(wbufs[0][:].rearrange("p (c m) -> p c m", m=128),
              gw1[cc].rearrange("(c p) m -> p c m", p=128), engine="gpsimd")
        p.bar()
        for kc in range(8):
            p.t.matmul(psA[:, 0:4], wbufs[0][:, kc * 128:(kc + 1) * 128],
                       pool[:].rearrange("p (b d) -> p b d", d=8)[:, :, kc],
                       start=(kc == 0), stop=(kc == 7))
        p.bar()
        p.v.tensor_scalar(gh[:].rearrange("p (cc b) -> p cc b", b=4)[:, cc, :],
                          psA[:, 0:4], gb1t[:, cc:cc + 1], 0.0, ALU.add, ALU.max)
        p.bar()
    for cc in range(2):
        p.dma(vbuf[0:128, 0:8], gw2[cc], engine="gpsimd")
        p.bar()
        p.t.matmul(psB[0:8, 0:4], vbuf[0:128, 0:8],
                   gh[:].rearrange("p (cc b) -> p cc b", b=4)[:, cc, :],
                   start=(cc == 0), stop=(cc == 1))
        p.bar()
    p.v.tensor_scalar(logit[0:8, 0:4], psB[0:8, 0:4], gb2t[0:8, 0:1], None, ALU.add)
    p.bar()
    p.t.transpose(psA[0:4, 0:8], logit[0:8, 0:4], idt[0:8, 0:8])
    p.bar()
    p.v.tensor_copy(tm[0:4, 0:8], psA[0:4, 0:8])
    p.bar()
    p.v.reduce_max(m1[0:4, 0:1], tm[0:4, 0:8], AX.X)
    p.v.reduce_max(nm1[0:4, 0:1], tm[0:4, 0:8], AX.X, negate=True)
    p.bar()
    p.v.tensor_scalar(tm2[0:4, 0:8], tm[0:4, 0:8], m1[0:4, 0:1], None, ALU.is_ge)
    p.bar()
    p.v.tensor_scalar(tm2[0:4, 0:8], tm2[0:4, 0:8], -1e30, None, ALU.mult)
    p.bar()
    p.v.tensor_tensor(tm2[0:4, 0:8], tm2[0:4, 0:8], tm[0:4, 0:8], ALU.add)
    p.bar()
    p.v.reduce_max(m2[0:4, 0:1], tm2[0:4, 0:8], AX.X)
    p.bar()
    p.v.tensor_scalar(tm2[0:4, 0:8], tm[0:4, 0:8], m2[0:4, 0:1], None, ALU.is_ge)
    p.v.tensor_tensor(dcol[0:4, 0:1], m2[0:4, 0:1], nm1[0:4, 0:1], ALU.add)
    p.bar()
    p.s.activation(dcol[0:4, 0:1], dcol[0:4, 0:1], AF.Exp)
    p.s.activation(tm[0:4, 0:8], tm[0:4, 0:8], AF.Exp, bias=nm1[0:4, 0:1])
    p.bar()
    p.v.tensor_scalar(dcol[0:4, 0:1], dcol[0:4, 0:1], 1.0, None, ALU.add)
    p.bar()
    p.v.reciprocal(dcol[0:4, 0:1], dcol[0:4, 0:1])
    p.bar()
    p.v.tensor_tensor(tm[0:4, 0:8], tm[0:4, 0:8], tm2[0:4, 0:8], ALU.mult)
    p.bar()
    p.v.tensor_scalar(tm[0:4, 0:8], tm[0:4, 0:8], dcol[0:4, 0:1], None, ALU.mult)
    p.bar()
    p.dma(o_w8[:], tm[0:4, 0:8])
    p.v.tensor_copy(tmbf[0:4, 0:8], tm[0:4, 0:8])
    p.bar()
    p.t.matmul(psB[0:1, 0:8], bselt[0:4, 0:1], tmbf[0:4, 0:8], start=True, stop=True)
    p.bar()
    p.v.tensor_copy(wselr[0:1, 0:8], psB[0:1, 0:8])
    p.v.tensor_copy(wselb[0:1, 0:8], psB[0:1, 0:8])
    p.bar()
    p.t.matmul(psB[0:16, 8:16], onesb[0:1, 0:16], wselb[0:1, 0:8], start=True, stop=True)
    p.bar()
    p.v.tensor_copy(wselc[0:16, 0:8], psB[0:16, 8:16])
    p.bar()

    def evac_win(oc, ib, sl, ps):
        if oc < 16:
            p.v.tensor_copy(ot[:, sl], ps)
        else:
            p.v.tensor_scalar(ot[:, sl], ps, 0.125, None, ALU.mult)

    def dma_win(oc, ib, sl):
        tgt = o_k1 if oc < 8 else (o_v1 if oc < 16 else o_q1)
        p.dma(tgt[oc % 8][:, sl], ot[:, sl])

    stream_proj(p, win, 1024, 3072, [t[:] for t in xnt], T, wbufs, psA, evac_win, dma_win)

    for pi, (wm, vt, ut, outd, scale) in enumerate([
            (wmq, vtq, utq, o_qf, 0.125), (wmk, vtk, utk, o_kf, 1.0), (wmv, vtv, utv, o_vf, 1.0)]):
        p.v.tensor_tensor(scol[0:16, 0:8], smt[0:16, pi * 8:(pi + 1) * 8], wselc[0:16, 0:8],
                          ALU.mult)
        p.bar()
        def evac(oc, ib, sl, ps, outd=outd, pi=pi, scale=scale):
            if scale != 1.0:
                p.v.tensor_scalar(ot[:, sl], ps, bqt[:, pi * 8 + oc:pi * 8 + oc + 1],
                                  scale, ALU.add, ALU.mult)
            else:
                p.v.tensor_scalar(ot[:, sl], ps, bqt[:, pi * 8 + oc:pi * 8 + oc + 1],
                                  None, ALU.add)

        def dmaf(oc, ib, sl, outd=outd):
            p.dma(outd[oc][:, sl], ot[:, sl])

        emit_svd_moe(p, [t[:] for t in xnt], T, wm, vt, ut, scol, None, evac,
                     wbufs, vbuf, ubuf, xvst, psA, psB, dma_cb=dmaf)
    sc.close()
    if fused is None:
        p.finalize()
        return nc, ctx


def host_LD(inputs, xn1_by_core, pp_by_core, S):
    gw1b = prep_blocks(f32c(inputs["L1_ffn_gw1"].T))
    gb1 = np.ascontiguousarray(f32c(inputs["L1_ffn_gb1"]).reshape(2, 128).T)
    gw2b = np.ascontiguousarray(f32c(inputs["L1_ffn_gw2"].T).reshape(2, 128, 8))
    gb2 = f32c(inputs["L1_ffn_gb2"]).reshape(8, 1)
    winb = prep_blocks(f32c(inputs["L1_attn_w_in"]))
    ppall = np.ascontiguousarray(np.stack(pp_by_core, axis=0))
    wm, vt, ut, sm, bb = {}, {}, {}, {}, {}
    for pi, pref in enumerate("qkv"):
        wm[pref] = prep_blocks(f32c(inputs[f"L1_ffn_{pref}Wm"].T))
        V = f32c(inputs[f"L1_ffn_{pref}V"])
        U = f32c(inputs[f"L1_ffn_{pref}U"])
        vt[pref] = np.ascontiguousarray(V.transpose(0, 2, 1))
        ut[pref] = np.ascontiguousarray(U.transpose(0, 2, 1))
        sm[pref] = f32c(inputs[f"L1_ffn_{pref}S"])
        bb[pref] = f32c(inputs[f"L1_ffn_{pref}b"])
    smat = np.zeros((16, 24), np.float32)
    bq = np.zeros((128, 24), np.float32)
    for pi, pref in enumerate("qkv"):
        smat[:, pi * 8:(pi + 1) * 8] = sm[pref].T
        bq[:, pi * 8:(pi + 1) * 8] = bb[pref].reshape(8, 128).T
    iden = np.eye(128, dtype=np.float32)
    maps = []
    for c in range(NCORES):
        b = c // 2
        bsel = np.zeros((4, 1), np.float32)
        bsel[b, 0] = 1.0
        maps.append(dict(xnd=xn1_by_core[c], ppd=ppall, gw1=gw1b, gb1=gb1, gw2=gw2b,
                         gb2=gb2, win=winb, wmq=wm["q"], wmk=wm["k"], wmv=wm["v"],
                         vtq=vt["q"], vtk=vt["k"], vtv=vt["v"], utq=ut["q"],
                         utk=ut["k"], utv=ut["v"], smat=smat, bq=bq, iden=iden,
                         bsel=bsel))
    return maps


def build_LE(S=4096, fused=None):
    SR = 2 * S - 1
    T_ = S // 2
    nkt = S // 128
    nqb = (S + 511) // 512
    if fused is None:
        nc = bass.Bass("TRN2", target_bir_lowering=False)
        ctx = ExitStack()
        p = Prog(nc, ctx)
        dov, pfx = {}, ""
    else:
        p, dov, pfx = fused
        nc, ctx = p.nc, None

    def _dt(name, shape, dtype, kind="ExternalInput"):
        if name in dov:
            return dov[name]
        return nc.dram_tensor(pfx + name, shape, dtype, kind=kind).ap()
    sc = p.sb_scope()
    fo = dov.get("_fo", 0)
    k1d = _dt("k1d", [2, 4, 128, T_], BF16)
    v1d = _dt("v1d", [2, 4, 128, T_], BF16)
    q1d = _dt("q1d", [2, 4, 128, T_], BF16)
    qfd = _dt("qfd", [2, 4, 128, T_], BF16)
    kfd = _dt("kfd", [2, 4, 128, T_], BF16)
    vfd = _dt("vfd", [2, 4, 128, T_], BF16)
    ampT = _dt("ampT", [21, 8], F32, kind="ExternalInput")
    shT = _dt("shT", [21, 8], F32, kind="ExternalInput")
    offT = _dt("offT", [21, 8], F32, kind="ExternalInput")
    relrep = _dt("relrep", [21, SR], F32, kind="ExternalInput")
    jmat = _dt("jmat", [128, 128], F32, kind="ExternalInput")
    idn = _dt("idn", [128, 128], F32, kind="ExternalInput")
    o_c0 = _dt("o_c0", [2, 4, 128, T_], BF16, kind="ExternalOutput")
    o_c1 = _dt("o_c1", [2, 4, 128, T_], BF16, kind="ExternalOutput")
    srd = _dt("srd", [8, SR], BF16, kind="Internal")

    qplus = sc.sb("qplus", (65, S), BF16)
    kplus = sc.sb("kplus", (65, S), BF16)
    vtmp = sc.sb("vtmp", (64, S), BF16)
    vpt = sc.sb("vpt", (128, nkt * 65), BF16)
    Hs = sc.sb("Hs", (128, 2 * S - 128), BF16)
    pt = sc.sb("pt", (128, nkt * 512), BF16)
    sq64 = sc.sb("sq64", (64, S), F32)
    SRC = min(2048, SR)
    relsb = sc.sb("relsb", (21, SRC), F32)
    srelsq = sc.sb("srelsq", (21, SRC), F32)
    srelex = sc.sb("srelex", (21, SRC), F32)
    srelbf = sc.sb("srelbf", (1, SRC), BF16)
    qn2 = sc.sb("qn2", (1, S), F32)
    kn2 = sc.sb("kn2", (1, S), F32)
    km = sc.sb("km", (1, 1), F32)
    mbs = sc.sb("mbs", (1, 8), F32)
    negoff = sc.sb("negoff", (21, 8), F32)
    nsh = sc.sb("nsh", (21, 8), F32)
    ampc = sc.sb("ampc", (21, 8), F32)
    jbf = sc.sb("jbf", (128, 128), BF16)
    idb = sc.sb("idb", (128, 128), BF16)
    onesc = sc.sb("onesc", (128, 1), F32)
    onesr64 = sc.sb("onesr64", (1, 64), F32)
    rrow = sc.sb("rrow", (1, 512), F32)
    rbsb = sc.sb("rbsb", (64, 512), F32)
    ctxo = sc.sb("ctxo", (64, 512), BF16)
    psA, psB = p.psA, p.psB

    p.dma(ampc[:], ampT)
    p.dma(jbf[:], jmat, engine="gpsimd")
    p.dma(idb[:], idn, engine="gpsimd")
    p.dma(negoff[:], offT)
    p.dma(nsh[:], shT)
    p.v.memset(onesc[:], 1.0)
    p.v.memset(onesr64[:], 1.0)
    p.v.memset(mbs[:], 0.0)
    p.bar()
    p.v.tensor_scalar(negoff[:], negoff[:], -1.0, None, ALU.mult)
    p.s.activation(nsh[:], nsh[:], AF.Abs)
    p.bar()
    p.v.tensor_scalar(nsh[:], nsh[:], -1.0, None, ALU.mult)
    p.bar()
    for c0 in range(0, SR, SRC):
        cw = min(SRC, SR - c0)
        p.dma(relsb[:, 0:cw], relrep[:, c0:c0 + cw])
        p.bar()
        for h in range(8):
            p.s.activation(srelsq[:, 0:cw], relsb[:, 0:cw], AF.Square, bias=negoff[0:21, h:h + 1])
            p.bar()
            p.s.activation(srelex[:, 0:cw], srelsq[:, 0:cw], AF.Exp, scale=nsh[0:21, h:h + 1])
            p.bar()
            for j0 in range(0, cw, 512):
                w = min(512, cw - j0)
                p.t.matmul(psA[0:1, j0:j0 + w], ampc[0:21, h:h + 1],
                           srelex[0:21, j0:j0 + w], start=True, stop=True)
            p.bar()
            for j0 in range(0, cw, 512):
                w = min(512, cw - j0)
                p.v.tensor_copy(srelbf[0:1, j0:j0 + w], psA[0:1, j0:j0 + w])
            p.bar()
            p.v.reduce_max(km[0:1, 0:1], srelbf[0:1, 0:cw], AX.X, apply_absolute_value=True)
            p.dma(srd[h][None, c0:c0 + cw], srelbf[0:1, 0:cw])
            p.bar()
            p.v.tensor_tensor(mbs[0:1, h:h + 1], mbs[0:1, h:h + 1], km[0:1, 0:1], ALU.max)
            p.bar()

    for blk in range(2):
        for h in range(8):
            ht, hr = h // 2, (h % 2) * 64
            qsrc = q1d if blk == 0 else qfd
            ksrc = k1d if blk == 0 else kfd
            vsrc = v1d if blk == 0 else vfd
            for j in range(2):
                p.dma(qplus[0:64, j * T_:(j + 1) * T_], qsrc[j, fo + ht][hr:hr + 64, :])
                p.dma(kplus[0:64, j * T_:(j + 1) * T_], ksrc[j, fo + ht][hr:hr + 64, :])
                p.dma(vtmp[0:64, j * T_:(j + 1) * T_], vsrc[j, fo + ht][hr:hr + 64, :])
            if blk == 0:
                stair = bass.AP(srd.tensor, h * SR, [[1, 128], [1, 2 * S - 128]])
                p.dma(Hs[:], stair)
            p.v.memset(kplus[64:65, 0:S], 1.0)
            p.v.memset(vpt[:], 1.0)
            p.bar()
            p.s.activation(sq64[0:64, 0:S], qplus[0:64, 0:S], AF.Square)
            p.bar()
            for g0 in range(0, S, 2048):
                nblk = min(4, (S - g0 + 511) // 512)
                for j in range(nblk):
                    c0 = g0 + j * 512
                    w = min(512, S - c0)
                    p.t.matmul(psA[0:1, j * 512:j * 512 + w], onesc[0:64, 0:1],
                               sq64[0:64, c0:c0 + w], start=True, stop=True)
                p.bar()
                for j in range(nblk):
                    c0 = g0 + j * 512
                    w = min(512, S - c0)
                    p.v.tensor_copy(qn2[0:1, c0:c0 + w], psA[0:1, j * 512:j * 512 + w])
                p.bar()
            p.s.activation(sq64[0:64, 0:S], kplus[0:64, 0:S], AF.Square)
            p.bar()
            for g0 in range(0, S, 2048):
                nblk = min(4, (S - g0 + 511) // 512)
                for j in range(nblk):
                    c0 = g0 + j * 512
                    w = min(512, S - c0)
                    p.t.matmul(psA[0:1, j * 512:j * 512 + w], onesc[0:64, 0:1],
                               sq64[0:64, c0:c0 + w], start=True, stop=True)
                p.bar()
                for j in range(nblk):
                    c0 = g0 + j * 512
                    w = min(512, S - c0)
                    p.v.tensor_copy(kn2[0:1, c0:c0 + w], psA[0:1, j * 512:j * 512 + w])
                p.bar()
            p.v.reduce_max(km[0:1, 0:1], kn2[0:1, 0:S], AX.X)
            p.bar()
            p.s.activation(km[0:1, 0:1], km[0:1, 0:1], AF.Sqrt)
            p.bar()
            p.s.activation(qn2[0:1, 0:S], qn2[0:1, 0:S], AF.Sqrt)
            p.bar()
            p.v.tensor_single_scalar(qn2[0:1, 0:S], qn2[0:1, 0:S], km[0:1, 0:1], ALU.mult)
            p.bar()
            if blk == 0:
                p.v.tensor_single_scalar(qn2[0:1, 0:S], qn2[0:1, 0:S], mbs[0:1, h:h + 1], ALU.add)
                p.bar()
            p.v.tensor_scalar(qplus[64:65, 0:S], qn2[0:1, 0:S], -1.0, None, ALU.mult)
            p.bar()
            for kt0 in range(0, nkt, 4):
                n4 = min(4, nkt - kt0)
                for j in range(n4):
                    kt = kt0 + j
                    p.t.transpose(psA[:, j * 512:j * 512 + 32].bitcast(BF16)[0:128, 0:64],
                                  vtmp[0:64, kt * 128:(kt + 1) * 128], idb[0:64, 0:64])
                p.bar()
                for j in range(n4):
                    kt = kt0 + j
                    p.v.tensor_copy(vpt[:, kt * 65:kt * 65 + 64],
                                    psA[:, j * 512:j * 512 + 32].bitcast(BF16)[0:128, 0:64])
                p.bar()
            for qb in range(nqb):
                q0 = qb * 512
                w = min(512, S - q0)
                for ktg in range(0, nkt, 8):
                    ng = min(8, nkt - ktg)
                    for j in range(ng):
                        kt = ktg + j
                        ps = psA if j < 4 else psB
                        jj = j % 4
                        p.t.matmul(ps[:, jj * 512:jj * 512 + w], kplus[:, kt * 128:(kt + 1) * 128],
                                   qplus[:, q0:q0 + w], start=True, stop=(blk == 1))
                        if blk == 0:
                            u0 = (S - 128) - kt * 128 + q0
                            p.t.matmul(ps[:, jj * 512:jj * 512 + w], jbf[:],
                                       Hs[:, u0:u0 + w], start=False, stop=True)
                    p.bar()
                    for half in range(2):
                        lo = ktg + half * 4
                        nh = min(4, nkt - lo) if lo < nkt else 0
                        if nh <= 0:
                            continue
                        ps = psA if half == 0 else psB
                        if w == 512 and nh == 4:
                            p.s.activation(pt[:, lo * 512:(lo + 4) * 512], ps[:, 0:2048], AF.Exp)
                        else:
                            for j in range(nh):
                                p.s.activation(pt[:, (lo + j) * 512:(lo + j) * 512 + w],
                                               ps[:, j * 512:j * 512 + w], AF.Exp)
                    p.bar()
                for kt in range(nkt):
                    p.t.matmul(psA[0:65, 0:w], vpt[:, kt * 65:(kt + 1) * 65],
                               pt[:, kt * 512:kt * 512 + w], start=(kt == 0), stop=(kt == nkt - 1))
                p.bar()
                p.v.reciprocal(rrow[0:1, 0:w], psA[64:65, 0:w])
                p.bar()
                p.t.matmul(psB[0:64, 0:w], onesr64[0:1, 0:64], rrow[0:1, 0:w],
                           start=True, stop=True)
                p.bar()
                p.v.tensor_copy(rbsb[0:64, 0:w], psB[0:64, 0:w])
                p.bar()
                p.v.tensor_tensor(ctxo[0:64, 0:w], psA[0:64, 0:w], rbsb[0:64, 0:w], ALU.mult)
                p.bar()
                octx = o_c0 if blk == 0 else o_c1
                for j in range(2):
                    lo = max(q0, j * T_)
                    hi = min(q0 + w, (j + 1) * T_)
                    if lo < hi:
                        p.dma(octx[j, ht][hr:hr + 64, lo - j * T_:hi - j * T_],
                              ctxo[0:64, lo - q0:hi - q0])
                p.bar()
    sc.close()
    if fused is None:
        p.finalize()
        return nc, ctx


def host_LE(inputs, k1, v1, q1, qf, kf, vf, S):
    import ml_dtypes
    T = S // 2
    amp = f32c(inputs["L1_attn_amplitudes"])
    sh = f32c(inputs["L1_attn_sharpness"])
    off = f32c(inputs["L1_attn_offsets"])
    rel = -np.arange(-(S - 1), S, dtype=np.float32)
    relrep = np.ascontiguousarray(np.broadcast_to(rel[None, :], (21, 2 * S - 1)))
    jm = np.eye(128, dtype=np.float32)[::-1].copy()
    idn = np.eye(128, dtype=np.float32)
    maps = []
    for c in range(NCORES):
        b, hg = c // 2, c % 2
        hsl = slice(hg * 8, (hg + 1) * 8)
        csl = slice(hg * 512, (hg + 1) * 512)

        def spl(x):
            a = x[b, csl].reshape(4, 128, S)
            return np.ascontiguousarray(np.stack([a[:, :, 0:T], a[:, :, T:S]])).astype(ml_dtypes.bfloat16)
        maps.append(dict(
            k1d=spl(k1), v1d=spl(v1), q1d=spl(q1), qfd=spl(qf), kfd=spl(kf), vfd=spl(vf),
            ampT=np.ascontiguousarray(amp[hsl].T), shT=np.ascontiguousarray(sh[hsl].T),
            offT=np.ascontiguousarray(off[hsl].T), relrep=relrep, jmat=jm, idn=idn))
    return maps


def build_LF(S=4096, fused=None):
    T = S // 2
    if fused is None:
        nc = bass.Bass("TRN2", target_bir_lowering=False)
        ctx = ExitStack()
        p = Prog(nc, ctx)
        dov, pfx = {}, ""
    else:
        p, dov, pfx = fused
        nc, ctx = p.nc, None

    def _dt(name, shape, dtype, kind="ExternalInput"):
        if name in dov:
            return dov[name]
        return nc.dram_tensor(pfx + name, shape, dtype, kind=kind).ap()
    sc = p.sb_scope()
    fo = dov.get("_fo", 0)
    c0d = _dt("c0d", [2, 4, 128, T], BF16, kind="ExternalInput")
    cfd = _dt("cfd", [2, 4, 128, T], BF16, kind="ExternalInput")
    x1d = _dt("x1d", [8, 128, T], F32, kind="ExternalInput")
    wga = _dt("wga", [16, 1024, 128], F32, kind="ExternalInput")
    bgc = _dt("bgc", [128, 16], F32, kind="ExternalInput")
    wmo = _dt("wmo", [8, 1024, 128], F32, kind="ExternalInput")
    vto = _dt("vto", [8, 1024, 16], F32, kind="ExternalInput")
    uto = _dt("uto", [8, 16, 1024], F32, kind="ExternalInput")
    smato = _dt("smato", [16, 8], F32, kind="ExternalInput")
    bo = _dt("bo", [128, 8], F32, kind="ExternalInput")
    w8d = _dt("w8d", [4, 8], F32, kind="ExternalInput")
    bsel = _dt("bsel", [4, 1], F32, kind="ExternalInput")
    o_x2 = _dt("o_x2", [8, 128, T], F32, kind="ExternalOutput")
    o_xn2 = _dt("o_xn2", [8, 128, T], BF16, kind="ExternalOutput")

    c0t = [sc.sb(f"c0_{i}", (128, T), BF16) for i in range(8)]
    cft = [sc.sb(f"cf_{i}", (128, T), BF16) for i in range(8)]
    x2t = [sc.sb(f"x2_{i}", (128, T), F32) for i in range(8)]
    bgt = sc.sb("bgt", (128, 16), F32)
    bot = sc.sb("bot", (128, 8), F32)
    smt = sc.sb("smt", (16, 8), F32)
    scol = sc.sb("scol", (16, 8), F32)
    w8t = sc.sb("w8t", (4, 8), BF16)
    bselt = sc.sb("bselt", (4, 1), BF16)
    wselc = sc.sb("wselc", (16, 8), F32)
    onesb = sc.sb("onesb", (1, 16), BF16)
    psA, psB = p.psA, p.psB
    _sc = p.sb_scope()
    gsig = _sc.sb("gsig", (128, T), F32)
    arow = _sc.sb("arow", (128, T), F32)
    tmp = _sc.sb("tmp", (128, T), F32)
    wbufs = [_sc.sb(f"wb{i}", (128, 1024), BF16) for i in range(2)]
    wbufs2 = [_sc.sb(f"wb2_{i}", (128, 1024), BF16) for i in range(2)]
    vbuf = _sc.sb("vbuf", (128, 128), BF16)
    ubuf = _sc.sb("ubuf", (16, 1024), BF16)
    xvst = [_sc.sb(f"xvs{e}", (16, T), BF16) for e in range(8)]
    p.v.memset(onesb[:], 1.0)
    for i in range(8):
        p.dma(c0t[i][:], c0d[i // 4, fo + i % 4])
        p.dma(cft[i][:], cfd[i // 4, fo + i % 4])
        p.dma(x2t[i][:], x1d[i])
    p.dma(bgt[:], bgc)
    p.dma(bot[:], bo)
    p.dma(smt[:], smato)
    p.dma(w8t[:], w8d, engine="gpsimd")
    p.dma(bselt[:], bsel, engine="gpsimd")
    p.bar()
    p.t.matmul(psB[0:1, 0:8], bselt[0:4, 0:1], w8t[0:4, 0:8], start=True, stop=True)
    p.bar()
    p.v.tensor_copy(w8t[0:1, 0:8], psB[0:1, 0:8])
    p.bar()
    p.t.matmul(psB[0:16, 8:16], onesb[0:1, 0:16], w8t[0:1, 0:8], start=True, stop=True)
    p.bar()
    p.v.tensor_copy(wselc[0:16, 0:8], psB[0:16, 8:16])
    p.bar()
    p.v.tensor_tensor(scol[0:16, 0:8], smt[0:16, 0:8], wselc[0:16, 0:8], ALU.mult)
    p.bar()
    tb = tblocks(T)
    for i in range(8):
        p.dma(wbufs[0][:].rearrange("p (c m) -> p c m", m=128),
              wga[i].rearrange("(c p) m -> p c m", p=128), engine="gpsimd")
        p.dma(wbufs2[0][:].rearrange("p (c m) -> p c m", m=128),
              wga[i + 8].rearrange("(c p) m -> p c m", p=128), engine="gpsimd")
        p.bar()
        for ib, sl in enumerate(tb):
            w = sl.stop - sl.start
            for kc in range(8):
                p.t.matmul(psA[:, ib * 512:ib * 512 + w], wbufs[0][:, kc * 128:(kc + 1) * 128],
                           c0t[kc][:, sl], start=(kc == 0), stop=(kc == 7))
                p.t.matmul(psB[:, ib * 512:ib * 512 + w], wbufs2[0][:, kc * 128:(kc + 1) * 128],
                           c0t[kc][:, sl], start=(kc == 0), stop=(kc == 7))
        p.bar()
        for ib, sl in enumerate(tb):
            w = sl.stop - sl.start
            p.s.activation(gsig[:, sl], psB[:, ib * 512:ib * 512 + w], AF.Sigmoid,
                           bias=bgt[:, 8 + i:9 + i])
            p.v.tensor_scalar(arow[:, sl], psA[:, ib * 512:ib * 512 + w],
                              bgt[:, i:i + 1], None, ALU.add)
        p.bar()
        p.v.tensor_tensor(tmp[:, 0:T], arow[:, 0:T], gsig[:, 0:T], ALU.mult)
        p.bar()
        p.v.tensor_tensor(x2t[i][:, 0:T], x2t[i][:, 0:T], tmp[:, 0:T], ALU.add)
        p.bar()
    def evac_o(oc, ib, sl, ps):
        p.v.tensor_scalar(tmp[:, sl], ps, bot[:, oc:oc + 1], None, ALU.add)

    def acc_o(oc, ib, sl):
        p.v.tensor_tensor(x2t[oc][:, sl], x2t[oc][:, sl], tmp[:, sl], ALU.add)

    emit_svd_moe(p, [t[:] for t in cft], T, wmo, vto, uto, scol, None, evac_o,
                 wbufs, vbuf, ubuf, xvst, psA, psB, dma_cb=acc_o)
    _sc.close()
    ln = make_ln(sc, T)
    init_consts(p, ln)
    for i in range(8):
        p.dma(o_x2[i], x2t[i][:])
    p.bar()
    xnb = c0t
    ln_full(p, [t[:] for t in x2t], T,
            dict(ln, stat_x=psA[0:1, 0:T], stat_x2=psB[0:1, 0:T], bc_ps=psB[:, 1024:1536]),
            out_bf=[t[:] for t in xnb])
    for i in range(8):
        p.dma(o_xn2[i], xnb[i][:])
    sc.close()
    if fused is None:
        p.finalize()
        return nc, ctx


def host_LF(inputs, ctx0_by_core, ctxf_by_core, x1_by_core, w8, S):
    wgab = prep_blocks(f32c(inputs["L1_attn_w_gate"]))
    bg = f32c(inputs["L1_attn_b_gate"])
    bgc = np.ascontiguousarray(bg.reshape(16, 128).T)
    wmo = prep_blocks(f32c(inputs["L1_ffn_oWm"].T))
    vto = np.ascontiguousarray(f32c(inputs["L1_ffn_oV"]).transpose(0, 2, 1))
    uto = np.ascontiguousarray(f32c(inputs["L1_ffn_oU"]).transpose(0, 2, 1))
    smato = np.ascontiguousarray(f32c(inputs["L1_ffn_oS"]).T)
    bo = np.ascontiguousarray(f32c(inputs["L1_ffn_ob"]).reshape(8, 128).T)
    maps = []
    for c in range(NCORES):
        b = c // 2
        bs = np.zeros((4, 1), np.float32)
        bs[b, 0] = 1.0
        maps.append(dict(c0d=ctx0_by_core[c].reshape(2, 4, 128, -1),
                         cfd=ctxf_by_core[c].reshape(2, 4, 128, -1), x1d=x1_by_core[c],
                         wga=wgab, bgc=bgc, wmo=wmo, vto=vto, uto=uto, smato=smato,
                         bo=bo, w8d=w8, bsel=bs))
    return maps





def build_LG(S=4096, fused=None):
    T = S // 2
    if fused is None:
        nc = bass.Bass("TRN2", target_bir_lowering=False)
        ctx = ExitStack()
        p = Prog(nc, ctx)
        dov, pfx = {}, ""
    else:
        p, dov, pfx = fused
        nc, ctx = p.nc, None

    def _dt(name, shape, dtype, kind="ExternalInput"):
        if name in dov:
            return dov[name]
        return nc.dram_tensor(pfx + name, shape, dtype, kind=kind).ap()
    sc = p.sb_scope()
    xnd = _dt("xnd", [8, 128, T], BF16, kind="ExternalInput")
    wip = _dt("wip", [32, 1024, 128], F32, kind="ExternalInput")
    o_xz = _dt("o_xz", [32, 128, T], BF16, kind="ExternalOutput")
    xnt = [sc.sb(f"sxn{i}", (128, T), BF16) for i in range(8)]
    ot = sc.sb("ot", (128, T), BF16)
    wbufs = [sc.sb(f"wb{i}", (128, 1024), BF16) for i in range(2)]
    psA = p.psA
    for i in range(8):
        p.dma(xnt[i][:], xnd[i])
    p.bar()

    def evac(oc, ib, sl, ps):
        p.v.tensor_copy(ot[:, sl], ps)

    def dmaf(oc, ib, sl):
        p.dma(o_xz[oc][:, sl], ot[:, sl])

    stream_proj(p, wip, 1024, 4096, [t[:] for t in xnt], T, wbufs, psA, evac, dmaf)
    sc.close()
    if fused is None:
        p.finalize()
        return nc, ctx


def host_LG(inputs, xn2_by_core, S):
    wipb = prep_blocks(f32c(inputs["L2_attn_in_proj_w"].T))
    return [dict(xnd=xn2_by_core[c], wip=wipb) for c in range(NCORES)]


def build_LH(S=4096, fused=None):
    if fused is None:
        nc = bass.Bass("TRN2", target_bir_lowering=False)
        ctx = ExitStack()
        p = Prog(nc, ctx)
        dov, pfx = {}, ""
    else:
        p, dov, pfx = fused
        nc, ctx = p.nc, None

    def _dt(name, shape, dtype, kind="ExternalInput"):
        if name in dov:
            return dov[name]
        return nc.dram_tensor(pfx + name, shape, dtype, kind=kind).ap()
    sc = p.sb_scope()
    xzd = _dt("xzd", [2, 16, 128, S // 2], BF16, kind="ExternalInput")
    cwc = _dt("cwc", [128, 64], F32, kind="ExternalInput")
    cbc = _dt("cbc", [128, 16], F32, kind="ExternalInput")
    xpw = _dt("xpw", [16, 128, 96], F32, kind="ExternalInput")
    dtw = _dt("dtw", [16, 64, 128], F32, kind="ExternalInput")
    dtb = _dt("dtb", [128, 16], F32, kind="ExternalInput")
    alg = _dt("alg", [128, 256], F32, kind="ExternalInput")
    dcl = _dt("dcl", [128, 16], F32, kind="ExternalInput")
    ind16 = _dt("ind16", [16, 2048], F32, kind="ExternalInput")
    o_y = _dt("o_y", [2, 16, 128, S // 2], F32, kind="ExternalOutput")
    xactd = _dt("xactd", [16, 128, S], BF16, kind="Internal")

    xab = [sc.sb(f"xab{i}", (128, S), BF16) for i in range(2)]
    xpad = sc.sb("xpad", (128, S + 3), BF16)
    cacc = sc.sb("cacc", (128, S), F32)
    cwt = sc.sb("cwt", (128, 64), F32)
    cbt = sc.sb("cbt", (128, 16), F32)
    xpt = sc.sb("xpt", (128, 16 * 96), BF16)
    dtwt = sc.sb("dtwt", (64, 16 * 128), BF16)
    dtbt = sc.sb("dtbt", (128, 16), F32)
    algt = sc.sb("algt", (128, 256), F32)
    dclt = sc.sb("dclt", (128, 16), F32)
    i16 = sc.sb("i16", (16, 2048), BF16)
    dblB = sc.sb("dblB", (16, S), BF16)
    dblC = sc.sb("dblC", (16, S), BF16)
    dbl64 = sc.sb("dbl64", (64, S), BF16)
    delta = sc.sb("delta", (128, S), F32)
    ut32 = sc.sb("ut32", (128, S), F32)
    du = sc.sb("du", (128, S), F32)
    dA = sc.sb("dA", (128, S), F32)
    dBu = sc.sb("dBu", (128, S), F32)
    hh = sc.sb("hh", (128, S), F32)
    yac = sc.sb("yac", (128, S), F32)
    bcB = sc.sb("bcB", (128, S), F32)
    psA, psB = p.psA, p.psB

    p.dma(cwt[:], cwc)
    p.dma(cbt[:], cbc)
    p.dma(xpt[:].rearrange("p (c m) -> p c m", m=96), xpw.rearrange("c p m -> p c m"), engine="gpsimd")
    p.dma(dtwt[:].rearrange("p (c m) -> p c m", m=128), dtw.rearrange("c p m -> p c m"), engine="gpsimd")
    p.dma(dtbt[:], dtb)
    p.dma(algt[:], alg)
    p.dma(dclt[:], dcl)
    p.dma(i16[:], ind16, engine="gpsimd")
    p.bar()
    p.s.activation(algt[:], algt[:], AF.Exp)
    p.bar()
    p.v.tensor_scalar(algt[:], algt[:], -1.0, None, ALU.mult)
    p.bar()
    for i in range(16):
        p.v.memset(xpad[:, 0:3], 0.0)
        for j in range(2):
            p.dma(xpad[:, 3 + j * (S // 2):3 + (j + 1) * (S // 2)], xzd[j, i])
        p.bar()
        p.v.tensor_scalar(cacc[:, 0:S], xpad[:, 0:S], cwt[:, i * 4:i * 4 + 1], None, ALU.mult)
        p.bar()
        for tap in range(1, 4):
            p.v.scalar_tensor_tensor(cacc[:, 0:S], xpad[:, tap:S + tap],
                                     cwt[:, i * 4 + tap:i * 4 + tap + 1], cacc[:, 0:S],
                                     ALU.mult, ALU.add)
            p.bar()
        p.s.activation(xab[0][:], cacc[:, 0:S], AF.Silu, bias=cbt[:, i:i + 1])
        p.bar()
        p.dma(xactd[i], xab[0][:])
        p.bar()
    for g0 in range(0, S, 1024):
        gw = min(1024, S - g0)
        for kc in range(16):
            p.dma(xab[kc % 2][:, 0:S], xactd[kc])
            p.bar()
            for sl in tblocks(gw):
                rh = xab[kc % 2][:, g0 + sl.start:g0 + sl.stop]
                p.t.matmul(psA[0:64, sl.start:sl.stop], xpt[:, kc * 96:kc * 96 + 64],
                           rh, start=(kc == 0), stop=(kc == 15))
                p.t.matmul(psA[0:16, 1024 + sl.start:1024 + sl.stop],
                           xpt[:, kc * 96 + 64:kc * 96 + 80], rh,
                           start=(kc == 0), stop=(kc == 15))
                p.t.matmul(psB[0:16, sl.start:sl.stop], xpt[:, kc * 96 + 80:kc * 96 + 96],
                           rh, start=(kc == 0), stop=(kc == 15))
        p.bar()
        p.v.tensor_copy(dbl64[0:64, g0:g0 + gw], psA[0:64, 0:gw])
        p.bar()
        p.v.tensor_copy(dblB[0:16, g0:g0 + gw], psA[0:16, 1024:1024 + gw])
        p.bar()
        p.v.tensor_copy(dblC[0:16, g0:g0 + gw], psB[0:16, 0:gw])
        p.bar()
    for i in range(16):
        p.dma(xab[0][:, 0:S], xactd[i])
        for g0 in range(0, S, 2048):
            gw = min(2048, S - g0)
            for sl in tblocks(gw):
                p.t.matmul(psA[:, sl.start:sl.stop], dtwt[0:64, i * 128:(i + 1) * 128],
                           dbl64[0:64, g0 + sl.start:g0 + sl.stop], start=True, stop=True)
            p.bar()
            p.v.tensor_scalar(dA[:, g0:g0 + gw], psA[:, 0:gw], dtbt[:, i:i + 1], None, ALU.add)
            p.bar()
        p.s.activation(dBu[:, 0:S], dA[:, 0:S], AF.Abs)
        p.bar()
        p.s.activation(dBu[:, 0:S], dBu[:, 0:S], AF.Exp, scale=p.cf(-1.0))
        p.bar()
        p.s.activation(dBu[:, 0:S], dBu[:, 0:S], AF.Ln, bias=p.cf(1.0))
        p.bar()
        p.v.tensor_scalar(delta[:, 0:S], dA[:, 0:S], 0.0, None, ALU.max)
        p.bar()
        p.v.tensor_tensor(delta[:, 0:S], delta[:, 0:S], dBu[:, 0:S], ALU.add)
        p.bar()
        p.v.tensor_copy(ut32[:, 0:S], xab[0][:])
        p.bar()
        p.v.tensor_tensor(du[:, 0:S], delta[:, 0:S], ut32[:, 0:S], ALU.mult)
        p.v.memset(yac[:, 0:S], 0.0)
        p.bar()
        for n in range(16):
            p.s.activation(dA[:, 0:S], delta[:, 0:S], AF.Exp,
                           scale=algt[:, i * 16 + n:i * 16 + n + 1])
            for g0 in range(0, S, 2048):
                gw = min(2048, S - g0)
                for sl in tblocks(gw):
                    p.t.matmul(psA[:, sl.start:sl.stop], i16[0:16, n * 128:(n + 1) * 128],
                               dblB[0:16, g0 + sl.start:g0 + sl.stop], start=True, stop=True)
                p.bar()
                p.v.tensor_copy(bcB[:, g0:g0 + gw], psA[:, 0:gw])
                p.bar()
            p.v.tensor_tensor(dBu[:, 0:S], du[:, 0:S], bcB[:, 0:S], ALU.mult)
            p.bar()
            p.v.tensor_tensor_scan(hh[:, 0:S], dA[:, 0:S], dBu[:, 0:S], 0.0, ALU.mult, ALU.add)
            p.bar()
            for g0 in range(0, S, 2048):
                gw = min(2048, S - g0)
                for sl in tblocks(gw):
                    p.t.matmul(psA[:, sl.start:sl.stop], i16[0:16, n * 128:(n + 1) * 128],
                               dblC[0:16, g0 + sl.start:g0 + sl.stop], start=True, stop=True)
                p.bar()
                p.v.tensor_copy(bcB[:, g0:g0 + gw], psA[:, 0:gw])
                p.bar()
            p.v.tensor_tensor(hh[:, 0:S], hh[:, 0:S], bcB[:, 0:S], ALU.mult)
            p.bar()
            p.v.tensor_tensor(yac[:, 0:S], yac[:, 0:S], hh[:, 0:S], ALU.add)
            p.bar()
        p.v.scalar_tensor_tensor(yac[:, 0:S], ut32[:, 0:S], dclt[:, i:i + 1], yac[:, 0:S],
                                 ALU.mult, ALU.add)
        p.bar()
        for j in range(2):
            p.dma(o_y[j, i], yac[:, j * (S // 2):(j + 1) * (S // 2)])
        p.bar()
    sc.close()
    if fused is None:
        p.finalize()
        return nc, ctx


def host_LH(inputs, xz_full, S):
    import ml_dtypes
    cw = f32c(inputs["L2_attn_conv_w"])
    cwc = np.ascontiguousarray(cw.reshape(16, 128, 4).transpose(1, 0, 2).reshape(128, 64))
    cbc = np.ascontiguousarray(f32c(inputs["L2_attn_conv_b"]).reshape(16, 128).T)
    alg_all = {}
    maps = []
    ind = np.zeros((16, 2048), np.float32)
    for n in range(16):
        ind[n, n * 128:(n + 1) * 128] = 1.0
    for c in range(NCORES):
        b, br = c // 2, c % 2
        sfx = "1" if br == 0 else "2"
        xp = f32c(inputs[f"L2_attn_x{sfx}_proj_w"])
        xpw = np.ascontiguousarray(xp.T.reshape(16, 128, 96))
        dtp = f32c(inputs[f"L2_attn_dt{sfx}_proj_w"])
        dtw = np.ascontiguousarray(dtp.T.reshape(64, 16, 128).transpose(1, 0, 2))
        dtb = np.ascontiguousarray(f32c(inputs[f"L2_attn_dt{sfx}_proj_b"]).reshape(16, 128).T)
        alg = f32c(inputs[f"L2_attn_A{sfx}_log"])
        algc = np.ascontiguousarray(alg.reshape(16, 128, 16).transpose(1, 0, 2).reshape(128, 256))
        dcl = np.ascontiguousarray(f32c(inputs[f"L2_attn_D{sfx}"]).reshape(16, 128).T)
        a = xz_full[b, 0:2048].reshape(16, 128, S)
        xzc = np.ascontiguousarray(np.stack([a[:, :, 0:S // 2], a[:, :, S // 2:]])).astype(ml_dtypes.bfloat16)
        maps.append(dict(xzd=xzc, cwc=cwc, cbc=cbc, xpw=xpw, dtw=dtw, dtb=dtb,
                         alg=algc, dcl=dcl, ind16=ind))
    return maps


def build_LI(S=4096, fused=None):
    T = S // 2
    if fused is None:
        nc = bass.Bass("TRN2", target_bir_lowering=False)
        ctx = ExitStack()
        p = Prog(nc, ctx)
        dov, pfx = {}, ""
    else:
        p, dov, pfx = fused
        nc, ctx = p.nc, None

    def _dt(name, shape, dtype, kind="ExternalInput"):
        if name in dov:
            return dov[name]
        return nc.dram_tensor(pfx + name, shape, dtype, kind=kind).ap()
    sc = p.sb_scope()
    fo = dov.get("_fo", 0)
    y1d = _dt("y1d", [16, 128, T], F32, kind="ExternalInput")
    y2d = _dt("y2d", [16, 128, T], F32, kind="ExternalInput")
    zd = _dt("zd", [16, 128, T], BF16, kind="ExternalInput")
    x2d = _dt("x2d", [8, 128, T], F32, kind="ExternalInput")
    wop = _dt("wop", [8, 2048, 128], F32, kind="ExternalInput")
    o_x3p = _dt("o_x3p", [8, 128, T], F32, kind="ExternalOutput")

    ygt = [sc.sb(f"yg{i}", (128, T), BF16) for i in range(16)]
    x2t = [sc.sb(f"x2_{i}", (128, T), F32) for i in range(8)]
    ta = sc.sb("ta", (128, T), F32)
    tbf = sc.sb("tbf", (128, T), F32)
    tsl = sc.sb("tsl", (128, T), F32)
    wbufs = [sc.sb(f"wb{i}", (128, 2048), BF16) for i in range(2)]
    psA = p.psA

    for i in range(8):
        p.dma(x2t[i][:], x2d[i])
    p.bar()
    for i in range(16):
        p.dma(ta[:, 0:T], y1d[fo + i])
        p.dma(tbf[:, 0:T], y2d[fo + i])
        p.dma(ygt[i][:], zd[i])
        p.bar()
        p.v.tensor_tensor(ta[:, 0:T], ta[:, 0:T], tbf[:, 0:T], ALU.subtract)
        p.s.activation(tsl[:, 0:T], ygt[i][:], AF.Silu)
        p.bar()
        p.v.tensor_tensor(ygt[i][:], ta[:, 0:T], tsl[:, 0:T], ALU.mult)
        p.bar()

    def evac(oc, ib, sl, ps):
        p.v.tensor_tensor(x2t[oc][:, sl], x2t[oc][:, sl], ps, ALU.add)

    def dmaf(oc, ib, sl):
        p.dma(o_x3p[oc][:, sl], x2t[oc][:, sl])

    stream_proj(p, wop, 2048, 1024, [t[:] for t in ygt], T, wbufs, psA, evac, dmaf)
    sc.close()
    if fused is None:
        p.finalize()
        return nc, ctx


def host_LI(inputs, y1_by_core, y2_by_core, z_by_core, x2_by_core, S):
    wopb = prep_blocks(f32c(inputs["L2_attn_out_proj_w"].T))
    return [dict(y1d=y1_by_core[c], y2d=y2_by_core[c], zd=z_by_core[c],
                 x2d=x2_by_core[c], wop=wopb) for c in range(NCORES)]


def build_LJ(S=4096, fused=None):
    T = S // 2
    TB = min(512, T)
    ntb = T // TB
    if fused is None:
        nc = bass.Bass("TRN2", target_bir_lowering=False)
        ctx = ExitStack()
        p = Prog(nc, ctx)
        dov, pfx = {}, ""
    else:
        p, dov, pfx = fused
        nc, ctx = p.nc, None

    def _dt(name, shape, dtype, kind="ExternalInput"):
        if name in dov:
            return dov[name]
        return nc.dram_tensor(pfx + name, shape, dtype, kind=kind).ap()
    sc = p.sb_scope()
    xnd = _dt("xnd", [8, 128, T], BF16, kind="ExternalInput")
    x3pd = _dt("x3pd", [8, 128, T], F32, kind="ExternalInput")
    w1d = _dt("w1d", [32, 1024, 128], F32, kind="ExternalInput")
    w2d = _dt("w2d", [32, 1024, 128], F32, kind="ExternalInput")
    h1d = _dt("h1d", [32, 4096, 128], F32, kind="ExternalInput")
    h2d = _dt("h2d", [32, 4096, 128], F32, kind="ExternalInput")
    w3d = _dt("w3d", [8, 4096, 128], F32, kind="ExternalInput")
    s12 = _dt("s12", [128, 64], F32, kind="ExternalInput")
    sh12 = _dt("sh12", [128, 64], F32, kind="ExternalInput")
    s3c = _dt("s3c", [128, 8], F32, kind="ExternalInput")
    o_x3 = _dt("o_x3", [8, 128, T], F32, kind="ExternalOutput")
    o_xn3 = _dt("o_xn3", [8, 128, T], BF16, kind="ExternalOutput")
    o_xn3f = _dt("o_xn3f", [8, 128, T], F32, kind="ExternalOutput")

    xnt = [sc.sb(f"sxn{i}", (128, T), BF16) for i in range(8)]
    x3t = [sc.sb(f"x3_{i}", (128, T), F32) for i in range(8)]
    s12t = sc.sb("s12t", (128, 64), F32)
    sh12t = sc.sb("sh12t", (128, 64), F32)
    s3t = sc.sb("s3t", (128, 8), F32)
    psA, psB = p.psA, p.psB
    _sc = p.sb_scope()
    ht = [_sc.sb(f"h_{i}", (128, TB), BF16) for i in range(32)]
    h1t = [_sc.sb(f"h1_{i}", (128, TB), BF16) for i in range(32)]
    tmp = _sc.sb("tmpj", (128, TB), F32)
    wbufs = [_sc.sb(f"wb{i}", (128, 1024), BF16) for i in range(2)]
    wbufs4 = [_sc.sb(f"wc{i}", (128, 4096), BF16) for i in range(2)]

    for i in range(8):
        p.dma(xnt[i][:], xnd[i])
        p.dma(x3t[i][:], x3pd[i])
    p.dma(s12t[:], s12)
    p.dma(sh12t[:], sh12)
    p.dma(s3t[:], s3c)
    p.bar()
    for tb in range(ntb):
        toff = tb * TB
        gsl = slice(toff, toff + TB)
        for oc in range(32):
            p.dma(wbufs[0][:].rearrange("p (c m) -> p c m", m=128),
                  w1d[oc].rearrange("(c p) m -> p c m", p=128), engine="gpsimd")
            p.dma(wbufs[1][:].rearrange("p (c m) -> p c m", m=128),
                  w2d[oc].rearrange("(c p) m -> p c m", p=128), engine="gpsimd")
            p.bar()
            for kc in range(8):
                p.t.matmul(psA[:, 0:TB], wbufs[0][:, kc * 128:(kc + 1) * 128],
                           xnt[kc][:, gsl], start=(kc == 0), stop=(kc == 7))
                p.t.matmul(psB[:, 0:TB], wbufs[1][:, kc * 128:(kc + 1) * 128],
                           xnt[kc][:, gsl], start=(kc == 0), stop=(kc == 7))
            p.bar()
            p.s.activation(tmp[:, 0:TB], psA[:, 0:TB], AF.Silu, scale=s12t[:, oc:oc + 1])
            p.v.tensor_scalar(ht[oc][:, 0:TB], psB[:, 0:TB], s12t[:, 32 + oc:33 + oc],
                              None, ALU.mult)
            p.bar()
            p.v.tensor_tensor(ht[oc][:, 0:TB], tmp[:, 0:TB], ht[oc][:, 0:TB], ALU.mult)
            p.bar()
        for oc in range(32):
            p.dma(wbufs4[oc % 2][:].rearrange("p (c m) -> p c m", m=128),
                  h1d[oc].rearrange("(c p) m -> p c m", p=128), engine="gpsimd")
            p.bar()
            for kc in range(32):
                p.t.matmul(psA[:, 0:TB], wbufs4[oc % 2][:, kc * 128:(kc + 1) * 128],
                           ht[kc][:, 0:TB], start=(kc == 0), stop=(kc == 31))
            p.bar()
            p.s.activation(h1t[oc][:, 0:TB], psA[:, 0:TB], AF.Gelu, scale=sh12t[:, oc:oc + 1])
            p.bar()
        for oc in range(32):
            p.dma(wbufs4[oc % 2][:].rearrange("p (c m) -> p c m", m=128),
                  h2d[oc].rearrange("(c p) m -> p c m", p=128), engine="gpsimd")
            p.bar()
            for kc in range(32):
                p.t.matmul(psA[:, 0:TB], wbufs4[oc % 2][:, kc * 128:(kc + 1) * 128],
                           h1t[kc][:, 0:TB], start=(kc == 0), stop=(kc == 31))
            p.bar()
            p.s.activation(ht[oc][:, 0:TB], psA[:, 0:TB], AF.Gelu, scale=sh12t[:, 32 + oc:33 + oc])
            p.bar()
        for oc in range(8):
            p.dma(wbufs4[oc % 2][:].rearrange("p (c m) -> p c m", m=128),
                  w3d[oc].rearrange("(c p) m -> p c m", p=128), engine="gpsimd")
            p.bar()
            for kc in range(32):
                p.t.matmul(psA[:, 0:TB], wbufs4[oc % 2][:, kc * 128:(kc + 1) * 128],
                           ht[kc][:, 0:TB], start=(kc == 0), stop=(kc == 31))
            p.bar()
            p.v.tensor_scalar(tmp[:, 0:TB], psA[:, 0:TB], s3t[:, oc:oc + 1], None, ALU.mult)
            p.bar()
            p.v.tensor_tensor(x3t[oc][:, gsl], x3t[oc][:, gsl], tmp[:, 0:TB], ALU.add)
            p.bar()
    _sc.close()
    ln = make_ln(sc, T)
    init_consts(p, ln)
    for i in range(8):
        p.dma(o_x3[i], x3t[i][:])
    p.bar()

    ln_full(p, [t[:] for t in x3t], T,
            dict(ln, stat_x=psA[0:1, 0:T], stat_x2=psB[0:1, 0:T], bc_ps=psB[:, 1024:1536]),
            out_bf=[t[:] for t in xnt], out_f32=[t[:] for t in x3t])
    for i in range(8):
        p.dma(o_xn3[i], xnt[i][:])
        p.dma(o_xn3f[i], x3t[i][:])
    sc.close()
    if fused is None:
        p.finalize()
        return nc, ctx


def host_LJ(inputs, xn2_by_core, x3p_by_core, S):
    w1b = prep_blocks(f32c(inputs["L2_ffn_w1_w"].T))
    w2b = prep_blocks(f32c(inputs["L2_ffn_w2_w"].T))
    h1b = prep_blocks(f32c(inputs["L2_ffn_h1_w"].T))
    h2b = prep_blocks(f32c(inputs["L2_ffn_h2_w"].T))
    w3b = prep_blocks(f32c(inputs["L2_ffn_w3_w"].T))
    s12 = np.concatenate([f32c(inputs["L2_ffn_w1_s"]).reshape(32, 128).T,
                          f32c(inputs["L2_ffn_w2_s"]).reshape(32, 128).T], axis=1)
    sh12 = np.concatenate([f32c(inputs["L2_ffn_h1_s"]).reshape(32, 128).T,
                           f32c(inputs["L2_ffn_h2_s"]).reshape(32, 128).T], axis=1)
    s3c = np.ascontiguousarray(f32c(inputs["L2_ffn_w3_s"]).reshape(8, 128).T)
    return [dict(xnd=xn2_by_core[c], x3pd=x3p_by_core[c], w1d=w1b, w2d=w2b,
                 h1d=h1b, h2d=h2b, w3d=w3b, s12=np.ascontiguousarray(s12),
                 sh12=np.ascontiguousarray(sh12), s3c=s3c) for c in range(NCORES)]


def build_LK(S=4096, fused=None):
    T = S // 2
    if fused is None:
        nc = bass.Bass("TRN2", target_bir_lowering=False)
        ctx = ExitStack()
        p = Prog(nc, ctx)
        dov, pfx = {}, ""
    else:
        p, dov, pfx = fused
        nc, ctx = p.nc, None

    def _dt(name, shape, dtype, kind="ExternalInput"):
        if name in dov:
            return dov[name]
        return nc.dram_tensor(pfx + name, shape, dtype, kind=kind).ap()
    sc = p.sb_scope()
    xnd = _dt("xnd", [8, 128, T], F32, kind="ExternalInput")
    nwb = _dt("nwb", [128, 16], F32, kind="ExternalInput")
    wip = _dt("wip", [32, 1024, 128], F32, kind="ExternalInput")
    o_xz = _dt("o_xz", [32, 128, T], BF16, kind="ExternalOutput")
    xt = [sc.sb(f"sx{i}", (128, T), F32) for i in range(8)]
    xlb = [sc.sb(f"xl{i}", (128, T), BF16) for i in range(8)]
    nwt = sc.sb("nwt", (128, 16), F32)
    ot = sc.sb("ot", (128, T), BF16)
    wbufs = [sc.sb(f"wb{i}", (128, 1024), BF16) for i in range(2)]
    ln = make_ln(sc, T)
    psA, psB = p.psA, p.psB
    init_consts(p, ln)
    for i in range(8):
        p.dma(xt[i][:], xnd[i])
    p.dma(nwt[:], nwb)
    p.bar()
    ln_full(p, [t[:] for t in xt], T,
            dict(ln, stat_x=psA[0:1, 0:T], stat_x2=psB[0:1, 0:T], bc_ps=psB[:, 0:512]),
            gamma=[nwt[:, 2 * i:2 * i + 1] for i in range(8)],
            beta=[nwt[:, 2 * i + 1:2 * i + 2] for i in range(8)],
            out_bf=[t[:] for t in xlb])

    def evac(oc, ib, sl, ps):
        p.v.tensor_copy(ot[:, sl], ps)

    def dmaf(oc, ib, sl):
        p.dma(o_xz[oc][:, sl], ot[:, sl])

    stream_proj(p, wip, 1024, 4096, [t[:] for t in xlb], T, wbufs, psA, evac, dmaf)
    sc.close()
    if fused is None:
        p.finalize()
        return nc, ctx


def host_LK(inputs, xn3_by_core, S):
    g = f32c(inputs["L3_attn_norm_w"]).reshape(8, 128)
    b = f32c(inputs["L3_attn_norm_b"]).reshape(8, 128)
    nwb = np.ascontiguousarray(np.stack([g, b], axis=2).transpose(1, 0, 2).reshape(128, 16))
    wipb = prep_blocks(f32c(inputs["L3_attn_in_proj_w"].T))
    return [dict(xnd=xn3_by_core[c], nwb=nwb, wip=wipb) for c in range(NCORES)]


def build_LL(S=4096, fused=None):
    if fused is None:
        nc = bass.Bass("TRN2", target_bir_lowering=False)
        ctx = ExitStack()
        p = Prog(nc, ctx)
        dov, pfx = {}, ""
    else:
        p, dov, pfx = fused
        nc, ctx = p.nc, None

    def _dt(name, shape, dtype, kind="ExternalInput"):
        if name in dov:
            return dov[name]
        return nc.dram_tensor(pfx + name, shape, dtype, kind=kind).ap()
    sc = p.sb_scope()
    xzd = _dt("xzd", [2, 16, 128, S // 2], BF16, kind="ExternalInput")
    chd = _dt("chd", [4, 8, 2048, 128], F32, kind="ExternalInput")
    cbc = _dt("cbc", [128, 8], F32, kind="ExternalInput")
    xp3 = _dt("xp3", [8, 128, 33], F32, kind="ExternalInput")
    o_xact = _dt("o_xact", [8, 128, S], BF16, kind="ExternalOutput")
    o_dbd = _dt("o_dbd", [1, S], F32, kind="ExternalOutput")
    o_dbB = _dt("o_dbB", [16, S], F32, kind="ExternalOutput")
    o_dbC = _dt("o_dbC", [16, S], F32, kind="ExternalOutput")

    xpadb = [sc.sb(f"xpad{i}", (128, S + 3), BF16) for i in range(2)]
    cbt = sc.sb("cbt", (128, 8), F32)
    xpt = sc.sb("xpt", (128, 8 * 33), BF16)
    xat = sc.sb("xat", (128, S), BF16)
    dbd = sc.sb("dbd", (1, S), F32)
    dbB = sc.sb("dbB", (16, S), F32)
    dbC = sc.sb("dbC", (16, S), F32)
    wbufs = [sc.sb(f"wb{i}", (128, 2048), BF16) for i in range(4)]
    psA, psB = p.psA, p.psB

    p.dma(cbt[:], cbc)
    p.dma(xpt[:].rearrange("p (c m) -> p c m", m=33), xp3.rearrange("c p m -> p c m"), engine="gpsimd")
    p.v.memset(dbd[:], 0.0)
    p.v.memset(dbB[:], 0.0)
    p.v.memset(dbC[:], 0.0)
    p.bar()
    for ot in range(8):
        for kc in range(16):
            p.v.memset(xpadb[kc % 2][:, 0:3], 0.0)
            for j in range(2):
                p.dma(xpadb[kc % 2][:, 3 + j * (S // 2):3 + (j + 1) * (S // 2)], xzd[j, kc])
            for tap in range(4):
                p.dma(wbufs[tap][:, kc * 128:(kc + 1) * 128],
                      chd[tap, ot][kc * 128:(kc + 1) * 128, :], engine="gpsimd")
            p.bar()
            for sl in tblocks(S):
                ps = psA if sl.start < 2048 else psB
                off = sl.start % 2048
                w = sl.stop - sl.start
                for tap in range(4):
                    p.t.matmul(ps[:, off:off + w], wbufs[tap][:, kc * 128:(kc + 1) * 128],
                               xpadb[kc % 2][:, tap + sl.start:tap + sl.stop],
                               start=(kc == 0 and tap == 0), stop=(kc == 15 and tap == 3))
            p.bar()
        for sl in tblocks(S):
            ps = psA if sl.start < 2048 else psB
            off = sl.start % 2048
            w = sl.stop - sl.start
            p.s.activation(xat[:, sl], ps[:, off:off + w], AF.Silu, bias=cbt[:, ot:ot + 1])
        p.bar()
        p.dma(o_xact[ot], xat[:, 0:S])
        for sl in tblocks(S):
            w = sl.stop - sl.start
            p.t.matmul(psA[0:1, 0:w], xpt[:, ot * 33:ot * 33 + 1], xat[:, sl],
                       start=True, stop=True)
            p.t.matmul(psA[0:16, 512:512 + w], xpt[:, ot * 33 + 1:ot * 33 + 17], xat[:, sl],
                       start=True, stop=True)
            p.t.matmul(psB[0:16, 0:w], xpt[:, ot * 33 + 17:ot * 33 + 33], xat[:, sl],
                       start=True, stop=True)
            p.bar()
            p.v.tensor_tensor(dbd[0:1, sl], dbd[0:1, sl], psA[0:1, 0:w], ALU.add)
            p.bar()
            p.v.tensor_tensor(dbB[0:16, sl], dbB[0:16, sl], psA[0:16, 512:512 + w], ALU.add)
            p.bar()
            p.v.tensor_tensor(dbC[0:16, sl], dbC[0:16, sl], psB[0:16, 0:w], ALU.add)
            p.bar()
    p.dma(o_dbd[:], dbd[:])
    p.dma(o_dbB[:], dbB[:])
    p.dma(o_dbC[:], dbC[:])
    sc.close()
    if fused is None:
        p.finalize()
        return nc, ctx


def host_LL(inputs, xz3_full, S):
    import ml_dtypes
    cw = f32c(inputs["L3_attn_conv_w"])
    xp = f32c(inputs["L3_attn_x_proj_w"])
    maps = []
    for c in range(NCORES):
        b, hf = c // 2, c % 2
        osl = slice(hf * 1024, (hf + 1) * 1024)
        chd = np.ascontiguousarray(
            cw[osl].transpose(2, 1, 0).reshape(4, 2048, 8, 128).transpose(0, 2, 1, 3))
        cbc = np.ascontiguousarray(f32c(inputs["L3_attn_conv_b"])[osl].reshape(8, 128).T)
        xp3 = np.ascontiguousarray(xp.T[osl].reshape(8, 128, 33))
        a = xz3_full[b, 0:2048].reshape(16, 128, S)
        xzc = np.ascontiguousarray(np.stack([a[:, :, 0:S // 2], a[:, :, S // 2:]])).astype(ml_dtypes.bfloat16)
        maps.append(dict(xzd=xzc, chd=chd, cbc=cbc, xp3=xp3))
    return maps


def build_LM(S=4096, fused=None):
    if fused is None:
        nc = bass.Bass("TRN2", target_bir_lowering=False)
        ctx = ExitStack()
        p = Prog(nc, ctx)
        dov, pfx = {}, ""
    else:
        p, dov, pfx = fused
        nc, ctx = p.nc, None

    def _dt(name, shape, dtype, kind="ExternalInput"):
        if name in dov:
            return dov[name]
        return nc.dram_tensor(pfx + name, shape, dtype, kind=kind).ap()
    sc = p.sb_scope()
    fo = dov.get("_fo", 0)
    xad = _dt("xad", [8, 128, S], BF16, kind="ExternalInput")
    zd = _dt("zd", [2, 8, 128, S // 2], BF16, kind="ExternalInput")
    dbdd = _dt("dbdd", [2, 1, S], F32, kind="ExternalInput")
    dbBd = _dt("dbBd", [2, 16, S], F32, kind="ExternalInput")
    dbCd = _dt("dbCd", [2, 16, S], F32, kind="ExternalInput")
    dtwr = _dt("dtwr", [1, 1024], F32, kind="ExternalInput")
    dtbc = _dt("dtbc", [128, 8], F32, kind="ExternalInput")
    ind16 = _dt("ind16", [16, 2048], F32, kind="ExternalInput")
    wop = _dt("wop", [8, 1024, 128], F32, kind="ExternalInput")
    o_ap = _dt("o_ap", [2, 8, 128, S // 2], F32, kind="ExternalOutput")
    ygd = _dt("ygd", [8, 128, S], BF16, kind="Internal")

    ygb = [sc.sb(f"ygb{i}", (128, S), BF16) for i in range(2)]
    dsb = sc.sb("dsb", (1, S), F32)
    dsb1 = sc.sb("dsb1", (1, S), F32)
    dsbf = sc.sb("dsbf", (1, S), BF16)
    dbB = sc.sb("dbB", (16, S), BF16)
    dbC = sc.sb("dbC", (16, S), BF16)
    tB = sc.sb("tB", (16, S), F32)
    dtwt = sc.sb("dtwt", (1, 1024), BF16)
    dtbt = sc.sb("dtbt", (128, 8), F32)
    i16 = sc.sb("i16", (16, 2048), BF16)
    dA = sc.sb("dA", (128, S), F32)
    dBu = sc.sb("dBu", (128, S), F32)
    ut32 = sc.sb("ut32", (128, S), F32)
    hh = sc.sb("hh", (128, S), F32)
    yac = sc.sb("yac", (128, S), F32)
    bcB = sc.sb("bcB", (128, S), F32)
    xab = sc.sb("xab", (128, S), BF16)
    wbufs = [sc.sb(f"wb{i}", (128, 1024), BF16) for i in range(2)]
    psA, psB = p.psA, p.psB

    p.dma(dtwt[:], dtwr, engine="gpsimd")
    p.dma(dtbt[:], dtbc)
    p.dma(i16[:], ind16, engine="gpsimd")
    p.dma(dsb[0:1, 0:S], dbdd[0])
    p.dma(dsb1[0:1, 0:S], dbdd[1])
    p.bar()
    p.v.tensor_tensor(dsb[0:1, 0:S], dsb[0:1, 0:S], dsb1[0:1, 0:S], ALU.add)
    p.bar()
    p.v.tensor_copy(dsbf[0:1, 0:S], dsb[0:1, 0:S])
    p.bar()
    p.dma(tB[0:16, 0:S], dbBd[0])
    p.bar()
    p.dma(bcB[0:16, 0:S], dbBd[1])
    p.bar()
    p.v.tensor_tensor(tB[0:16, 0:S], tB[0:16, 0:S], bcB[0:16, 0:S], ALU.add)
    p.bar()
    p.v.tensor_copy(dbB[0:16, 0:S], tB[0:16, 0:S])
    p.bar()
    p.dma(tB[0:16, 0:S], dbCd[0])
    p.bar()
    p.dma(bcB[0:16, 0:S], dbCd[1])
    p.bar()
    p.v.tensor_tensor(tB[0:16, 0:S], tB[0:16, 0:S], bcB[0:16, 0:S], ALU.add)
    p.bar()
    p.v.tensor_copy(dbC[0:16, 0:S], tB[0:16, 0:S])
    p.bar()
    for i in range(8):
        p.dma(xab[:, 0:S], xad[i])
        for g0 in range(0, S, 2048):
            gw = min(2048, S - g0)
            for sl in tblocks(gw):
                p.t.matmul(psA[:, sl.start:sl.stop], dtwt[0:1, i * 128:(i + 1) * 128],
                           dsbf[0:1, g0 + sl.start:g0 + sl.stop], start=True, stop=True)
            p.bar()
            p.v.tensor_scalar(dA[:, g0:g0 + gw], psA[:, 0:gw], dtbt[:, i:i + 1], None, ALU.add)
            p.bar()
        p.s.activation(dBu[:, 0:S], dA[:, 0:S], AF.Abs)
        p.bar()
        p.s.activation(dBu[:, 0:S], dBu[:, 0:S], AF.Exp, scale=p.cf(-1.0))
        p.bar()
        p.s.activation(dBu[:, 0:S], dBu[:, 0:S], AF.Ln, bias=p.cf(1.0))
        p.bar()
        p.v.tensor_scalar(dA[:, 0:S], dA[:, 0:S], 0.0, None, ALU.max)
        p.bar()
        p.v.tensor_tensor(dA[:, 0:S], dA[:, 0:S], dBu[:, 0:S], ALU.add)
        p.bar()
        p.s.activation(dA[:, 0:S], dA[:, 0:S], AF.Exp, scale=p.cf(-1.0))
        p.v.tensor_copy(ut32[:, 0:S], xab[:, 0:S])
        p.bar()
        p.v.memset(yac[:, 0:S], 0.0)
        p.bar()
        for n in range(16):
            for g0 in range(0, S, 2048):
                gw = min(2048, S - g0)
                for sl in tblocks(gw):
                    p.t.matmul(psA[:, sl.start:sl.stop], i16[0:16, n * 128:(n + 1) * 128],
                               dbB[0:16, g0 + sl.start:g0 + sl.stop], start=True, stop=True)
                p.bar()
                p.v.tensor_copy(bcB[:, g0:g0 + gw], psA[:, 0:gw])
                p.bar()
            p.v.tensor_tensor(dBu[:, 0:S], ut32[:, 0:S], bcB[:, 0:S], ALU.mult)
            p.bar()
            p.v.tensor_tensor_scan(hh[:, 0:S], dA[:, 0:S], dBu[:, 0:S], 0.0, ALU.mult, ALU.add)
            p.bar()
            for g0 in range(0, S, 2048):
                gw = min(2048, S - g0)
                for sl in tblocks(gw):
                    p.t.matmul(psA[:, sl.start:sl.stop], i16[0:16, n * 128:(n + 1) * 128],
                               dbC[0:16, g0 + sl.start:g0 + sl.stop], start=True, stop=True)
                p.bar()
                p.v.tensor_copy(bcB[:, g0:g0 + gw], psA[:, 0:gw])
                p.bar()
            p.v.tensor_tensor(hh[:, 0:S], hh[:, 0:S], bcB[:, 0:S], ALU.mult)
            p.bar()
            p.v.tensor_tensor(yac[:, 0:S], yac[:, 0:S], hh[:, 0:S], ALU.add)
            p.bar()
        for j in range(2):
            p.dma(xab[:, j * (S // 2):(j + 1) * (S // 2)], zd[j, fo + i])
        p.bar()
        p.s.activation(ut32[:, 0:S], xab[:, 0:S], AF.Silu)
        p.bar()
        p.v.tensor_tensor(ygb[0][:], yac[:, 0:S], ut32[:, 0:S], ALU.mult)
        p.bar()
        p.dma(ygd[i], ygb[0][:])
        p.bar()

    ot = dA
    for oc in range(8):
        p.dma(wbufs[0][:].rearrange("p (c m) -> p c m", m=128),
              wop[oc].rearrange("(c p) m -> p c m", p=128), engine="gpsimd")
        p.bar()
        for kc in range(8):
            p.dma(ygb[kc % 2][:], ygd[kc])
            p.bar()
            for sl in tblocks(S):
                ps = psA if sl.start < 2048 else psB
                off = sl.start % 2048
                w = sl.stop - sl.start
                p.t.matmul(ps[:, off:off + w], wbufs[0][:, kc * 128:(kc + 1) * 128],
                           ygb[kc % 2][:, sl], start=(kc == 0), stop=(kc == 7))
        p.bar()
        for sl in tblocks(S):
            ps = psA if sl.start < 2048 else psB
            off = sl.start % 2048
            w = sl.stop - sl.start
            p.v.tensor_copy(ot[:, sl], ps[:, off:off + w])
        p.bar()
        for j in range(2):
            p.dma(o_ap[j, oc], ot[:, j * (S // 2):(j + 1) * (S // 2)])
        p.bar()
    sc.close()
    if fused is None:
        p.finalize()
        return nc, ctx


def host_LM(inputs, xact_by_half, z_full, dbd, dbB, dbC, S):
    import ml_dtypes
    dtw = f32c(inputs["L3_attn_dt_w"])[:, 0]
    dtb = f32c(inputs["L3_attn_dt_b"])
    wopT = f32c(inputs["L3_attn_out_proj_w"].T)
    ind = np.zeros((16, 2048), np.float32)
    for n in range(16):
        ind[n, n * 128:(n + 1) * 128] = 1.0
    maps = []
    for c in range(NCORES):
        b, hf = c // 2, c % 2
        osl = slice(hf * 1024, (hf + 1) * 1024)
        maps.append(dict(
            xad=np.ascontiguousarray(xact_by_half[hf][b].reshape(8, 128, S)).astype(ml_dtypes.bfloat16),
            zd=np.ascontiguousarray(np.stack([z_full[b, osl, 0:S // 2].reshape(8, 128, S // 2),
                                              z_full[b, osl, S // 2:].reshape(8, 128, S // 2)])).astype(ml_dtypes.bfloat16),
            dbdd=np.ascontiguousarray(np.stack([dbd[(b, 0)], dbd[(b, 1)]])).reshape(2, 1, S),
            dbBd=np.ascontiguousarray(np.stack([dbB[(b, 0)], dbB[(b, 1)]])),
            dbCd=np.ascontiguousarray(np.stack([dbC[(b, 0)], dbC[(b, 1)]])),
            dtwr=np.ascontiguousarray(dtw[osl].reshape(1, 1024)),
            dtbc=np.ascontiguousarray(dtb[osl].reshape(8, 128).T),
            ind16=ind,
            wop=prep_blocks(np.ascontiguousarray(wopT[osl]))))
    return maps


_FP4_MIDS = [0.25, 0.75, 1.25, 1.75, 2.5, 3.5, 5.0]
_FP4_INCS = [0.5, 0.5, 0.5, 0.5, 1.0, 1.0, 2.0]


def emit_quant(p, out_bf, in_ap, W, scr):
    nb = W // 16
    inv = in_ap.rearrange("p (b k) -> p b k", k=16)
    p.v.reduce_max(scr["bm"][:, 0:nb], inv, AX.X, apply_absolute_value=True)
    p.bar()
    p.v.tensor_scalar(scr["bm"][:, 0:nb], scr["bm"][:, 0:nb], 1.0 / 6.0, 1e-12, ALU.mult, ALU.max)
    p.bar()
    escv = scr["esc"][:, 0:W].rearrange("p (b k) -> p b k", k=16)
    for j in range(16):
        p.v.tensor_copy(escv[:, :, j], scr["bm"][:, 0:nb])
        if j % 4 == 3:
            p.bar()
    p.bar()
    p.v.reciprocal(scr["qa"][:, 0:W], scr["esc"][:, 0:W])
    p.bar()
    p.v.tensor_tensor(scr["tq"][:, 0:W], in_ap, scr["qa"][:, 0:W], ALU.mult)
    p.bar()
    p.s.activation(scr["axn"][:, 0:W], scr["tq"][:, 0:W], AF.Abs)
    p.bar()
    p.s.sign(scr["tq"][:, 0:W], scr["tq"][:, 0:W])
    p.bar()
    p.v.tensor_scalar(scr["qa"][:, 0:W], scr["axn"][:, 0:W], _FP4_MIDS[0], _FP4_INCS[0],
                      ALU.is_ge, ALU.mult)
    p.bar()
    for k in range(1, 7):
        p.v.tensor_scalar(scr["esc2"][:, 0:W], scr["axn"][:, 0:W], _FP4_MIDS[k], _FP4_INCS[k],
                          ALU.is_ge, ALU.mult)
        p.bar()
        p.v.tensor_tensor(scr["qa"][:, 0:W], scr["qa"][:, 0:W], scr["esc2"][:, 0:W], ALU.add)
        p.bar()
    p.v.tensor_tensor(scr["qa"][:, 0:W], scr["qa"][:, 0:W], scr["tq"][:, 0:W], ALU.mult)
    p.bar()
    p.v.tensor_tensor(out_bf, scr["qa"][:, 0:W], scr["esc"][:, 0:W], ALU.mult)
    p.bar()


def build_LN(S=4096, fused=None):
    T = S // 2
    TB = 512
    if fused is None:
        nc = bass.Bass("TRN2", target_bir_lowering=False)
        ctx = ExitStack()
        p = Prog(nc, ctx)
        dov, pfx = {}, ""
    else:
        p, dov, pfx = fused
        nc, ctx = p.nc, None

    def _dt(name, shape, dtype, kind="ExternalInput"):
        if name in dov:
            return dov[name]
        return nc.dram_tensor(pfx + name, shape, dtype, kind=kind).ap()
    sc = p.sb_scope()
    fo = dov.get("_fo", 0)
    xn3d = _dt("xn3d", [8, 128, T], F32, kind="ExternalInput")
    x3d = _dt("x3d", [8, 128, T], F32, kind="ExternalInput")
    ap0d = _dt("ap0d", [8, 128, T], F32, kind="ExternalInput")
    ap1d = _dt("ap1d", [8, 128, T], F32, kind="ExternalInput")
    wgd = _dt("wgd", [22, 128, 1024], F32, kind="ExternalInput")
    wud = _dt("wud", [22, 128, 1024], F32, kind="ExternalInput")
    wdd = _dt("wdd", [8, 128, 2816], F32, kind="ExternalInput")
    idnd = _dt("idnd", [128, 128], F32, kind="ExternalInput")
    o_x4 = _dt("o_x4", [8, 128, T], F32, kind="ExternalOutput")
    gated = _dt("gated", [T // 128, 128, 2816], BF16, kind="Internal")
    hqtd = _dt("hqtd", [22, 128, T], BF16, kind="Internal")

    ntt = T // 128
    xqT = [sc.sb(f"xqT{i}", (128, T), BF16) for i in range(8)]
    wslot = [sc.sb(f"ws{i}", (128, 2816), BF16) for i in range(8)]
    scr = dict(bm=sc.sb("q_bm", (128, 176), F32), esc=sc.sb("q_esc", (128, 2816), F32),
               axn=sc.sb("q_axn", (128, 2816), F32), qa=sc.sb("q_qa", (128, 2816), F32),
               tq=sc.sb("q_tq", (128, 2816), F32), esc2=sc.sb("q_esc2", (128, 2816), F32))
    ft = sc.sb("ft", (128, 2816), F32)
    fbf = sc.sb("fbf", (128, 2816), BF16)
    idb = sc.sb("idb", (128, 128), BF16)
    idf = sc.sb("idf", (128, 128), F32)
    res = [sc.sb(f"res{i}", (128, 512), F32) for i in range(4)]
    strm = [sc.sb(f"st{i}", (128, 2816), BF16) for i in range(2)]
    psA, psB = p.psA, p.psB

    p.dma(idb[:], idnd, engine="gpsimd")
    p.dma(idf[:], idnd)
    p.bar()
    for tt in range(ntt):
        for dc in range(8):
            ps = psA if dc < 4 else psB
            p.dma(ft[:, dc * 128:(dc + 1) * 128], xn3d[dc][:, tt * 128:(tt + 1) * 128])
        p.bar()
        for dc in range(8):
            ps = psA if dc < 4 else psB
            p.t.transpose(ps[:, (dc % 4) * 512:(dc % 4) * 512 + 128],
                          ft[:, dc * 128:(dc + 1) * 128], idf[:])
        p.bar()
        for dc in range(8):
            ps = psA if dc < 4 else psB
            p.v.tensor_copy(scr["tq"][:, dc * 128:(dc + 1) * 128],
                            ps[:, (dc % 4) * 512:(dc % 4) * 512 + 128])
        p.bar()
        emit_quant(p, fbf[:, 0:1024], scr["tq"][:, 0:1024], 1024, scr)
        for dc in range(8):
            ps = psA if dc < 4 else psB
            p.t.transpose(ps[:, (dc % 4) * 512:(dc % 4) * 512 + 64].bitcast(BF16)[0:128, 0:128],
                          fbf[:, dc * 128:(dc + 1) * 128], idb[:])
        p.bar()
        for dc in range(8):
            ps = psA if dc < 4 else psB
            p.v.tensor_copy(xqT[dc][:, tt * 128:(tt + 1) * 128],
                            ps[:, (dc % 4) * 512:(dc % 4) * 512 + 64].bitcast(BF16)[0:128, 0:128])
        p.bar()

    def quant_wT(wdram, nt, wid, outslot_cb):
        for i in range(nt):
            p.dma(ft[:, 0:wid], wdram[i])
            p.bar()
            emit_quant(p, fbf[:, 0:wid], ft[:, 0:wid], wid, scr)
            nkc = wid // 128
            for kc0 in range(0, nkc, 8):
                n8 = min(8, nkc - kc0)
                for j in range(n8):
                    kc = kc0 + j
                    ps = psA if j < 4 else psB
                    p.t.transpose(ps[:, (j % 4) * 512:(j % 4) * 512 + 64].bitcast(BF16)[0:128, 0:128],
                                  fbf[:, kc * 128:(kc + 1) * 128], idb[:])
                p.bar()
                for j in range(n8):
                    kc = kc0 + j
                    ps = psA if j < 4 else psB
                    outslot_cb(kc, i, ps[:, (j % 4) * 512:(j % 4) * 512 + 64].bitcast(BF16)[0:128, 0:128])
                p.bar()

    def to_wslot(kc, i, src):
        p.v.tensor_copy(wslot[kc][:, i * 128:(i + 1) * 128], src)

    quant_wT(wgd, 22, 1024, to_wslot)
    for tt in range(ntt):
        for nb0 in range(0, 2816, 512):
            w = min(512, 2816 - nb0)
            ps = psA if (nb0 // 512) % 2 == 0 else psB
            for kc in range(8):
                p.t.matmul(ps[:, 0:w], xqT[kc][:, tt * 128:(tt + 1) * 128],
                           wslot[kc][:, nb0:nb0 + w], start=(kc == 0), stop=(kc == 7))
            p.bar()
            p.v.tensor_copy(fbf[:, nb0:nb0 + w], ps[:, 0:w])
            p.bar()
        p.dma(gated[tt], fbf[:, 0:2816])
        p.bar()
    quant_wT(wud, 22, 1024, to_wslot)
    for tt in range(ntt):
        for nb0 in range(0, 2816, 512):
            w = min(512, 2816 - nb0)
            ps = psA if (nb0 // 512) % 2 == 0 else psB
            for kc in range(8):
                p.t.matmul(ps[:, 0:w], xqT[kc][:, tt * 128:(tt + 1) * 128],
                           wslot[kc][:, nb0:nb0 + w], start=(kc == 0), stop=(kc == 7))
            p.bar()
            p.v.tensor_copy(ft[:, nb0:nb0 + w], ps[:, 0:w])
            p.bar()
        p.dma(fbf[:, 0:2816], gated[tt])
        p.bar()
        p.s.activation(scr["esc2"][:, 0:2816], fbf[:, 0:2816], AF.Silu)
        p.bar()
        p.v.tensor_tensor(ft[:, 0:2816], ft[:, 0:2816], scr["esc2"][:, 0:2816], ALU.mult)
        p.bar()
        emit_quant(p, fbf[:, 0:2816], ft[:, 0:2816], 2816, scr)
        for kc0 in range(0, 22, 8):
            n8 = min(8, 22 - kc0)
            for j in range(n8):
                kc = kc0 + j
                ps = psA if j < 4 else psB
                p.t.transpose(ps[:, (j % 4) * 512:(j % 4) * 512 + 64].bitcast(BF16)[0:128, 0:128],
                              fbf[:, kc * 128:(kc + 1) * 128], idb[:])
            p.bar()
            for j in range(n8):
                kc = kc0 + j
                ps = psA if j < 4 else psB
                p.v.tensor_copy(strm[0][:, j * 128:(j + 1) * 128],
                                ps[:, (j % 4) * 512:(j % 4) * 512 + 64].bitcast(BF16)[0:128, 0:128])
            p.bar()
            for j in range(n8):
                kc = kc0 + j
                p.dma(hqtd[kc][:, tt * 128:(tt + 1) * 128], strm[0][:, j * 128:(j + 1) * 128])
            p.bar()
    def to_wdslot(kc, i, src):
        p.v.tensor_copy(wslot[i][:, kc * 128:(kc + 1) * 128] if False else
                        wslot[kc % 8][:, 0:128], src)

    def to_wdslot2(kc, i, src):
        addr = kc * 1024 + i * 128
        p.v.tensor_copy(wslot[addr // 2816][:, addr % 2816:addr % 2816 + 128], src)

    quant_wT(wdd, 8, 2816, lambda kc, i, src: to_wdslot2(kc, i, src))
    for oc in range(8):
        for t0 in range(0, T, 512):
            w = min(512, T - t0)
            p.dma(res[0][:, 0:w], x3d[oc][:, t0:t0 + w])
            p.dma(res[1][:, 0:w], ap0d[fo + oc][:, t0:t0 + w])
            p.dma(res[2][:, 0:w], ap1d[fo + oc][:, t0:t0 + w])
            p.dma(res[3][:, 0:w], xn3d[oc][:, t0:t0 + w])
            for kc in range(22):
                p.dma(strm[kc % 2][:, 0:w], hqtd[kc][:, t0:t0 + w])
                p.bar()
                addr = kc * 1024 + oc * 128
                p.t.matmul(psA[:, 0:w], wslot[addr // 2816][:, addr % 2816:addr % 2816 + 128],
                           strm[kc % 2][:, 0:w], start=(kc == 0), stop=(kc == 21))
            p.bar()
            p.v.tensor_tensor(scr["tq"][:, 0:w], psA[:, 0:w], res[0][:, 0:w], ALU.add)
            p.bar()
            p.v.tensor_tensor(scr["tq"][:, 0:w], scr["tq"][:, 0:w], res[1][:, 0:w], ALU.add)
            p.bar()
            p.v.tensor_tensor(scr["tq"][:, 0:w], scr["tq"][:, 0:w], res[2][:, 0:w], ALU.add)
            p.bar()
            p.v.tensor_tensor(scr["tq"][:, 0:w], scr["tq"][:, 0:w], res[3][:, 0:w], ALU.add)
            p.bar()
            p.dma(o_x4[oc][:, t0:t0 + w], scr["tq"][:, 0:w])
            p.bar()
    sc.close()
    if fused is None:
        p.finalize()
        return nc, ctx


def host_LN(inputs, xn3_by_core, x3_by_core, ap0_by_core, ap1_by_core, S):
    wg = f32c(inputs["L3_ffn_w_gate"]).reshape(22, 128, 1024)
    wu = f32c(inputs["L3_ffn_w_up"]).reshape(22, 128, 1024)
    wd = f32c(inputs["L3_ffn_w_down"]).reshape(8, 128, 2816)
    idn = np.eye(128, dtype=np.float32)
    return [dict(xn3d=xn3_by_core[c], x3d=x3_by_core[c], ap0d=ap0_by_core[c],
                 ap1d=ap1_by_core[c], wgd=wg, wud=wu, wdd=wd, idnd=idn)
            for c in range(NCORES)]


PAIRS = [[0, 1], [2, 3], [4, 5], [6, 7]]
ALL8 = [[0, 1, 2, 3, 4, 5, 6, 7]]


def build_FUSED(S=4096):
    T = S // 2
    nc = bass.Bass("TRN2", target_bir_lowering=False)
    ctx = ExitStack()
    p = Prog(nc, ctx)
    rv = nc.partition_id() % 2

    def I(name, shape, dt_):
        return nc.dram_tensor("i_" + name, shape, dt_, kind="Internal").ap()

    def flat2(ap):
        return ap.rearrange("t p T -> (t p) T")

    def AG(in2d, out2d, groups=PAIRS):
        p.bar()
        p.cc("AllGather", ALU.bypass, groups, [in2d], [out2d])
        p.bar()

    def SEL(dst, src_tensor_ap, base_sym, dims):
        p.bar()
        p.dma(dst, bass.AP(src_tensor_ap.tensor, base_sym, dims))
        p.bar()

    i_xn0 = I("xn0", [8, 128, T], BF16)
    i_q = I("q", [8, 128, T], F32)
    i_v = I("v", [8, 128, T], F32)
    i_gs = I("gs", [8, 128, T], F32)
    g_q = I("gq", [2, 8, 128, T], F32)
    g_v = I("gv", [2, 8, 128, T], F32)
    l_q = I("lq", [2, 4, 128, T], F32)
    l_v = I("lv", [2, 4, 128, T], F32)
    i_y = I("y", [2, 4, 128, T], F32)
    g_y = I("gy", [2, 2, 4, 128, T], F32)
    l_y = I("ly", [2, 4, 128, T], F32)
    i_xp = I("xp", [8, 128, T], F32)
    i_x1 = I("x1", [8, 128, T], F32)
    i_xn1 = I("xn1", [8, 128, T], BF16)
    i_pp = I("pp", [128, 8], F32)
    g_pp = I("gpp", [8, 128, 8], F32)
    qkv_names = ["k1", "v1", "q1", "qf", "kf", "vf"]
    i_qkv = {n: I(n, [8, 128, T], BF16) for n in qkv_names}
    g_qkv = {n: I("g" + n, [2, 8, 128, T], BF16) for n in qkv_names}
    l_qkv = {n: I("l" + n, [2, 4, 128, T], BF16) for n in qkv_names}
    i_c0 = I("c0", [2, 4, 128, T], BF16)
    i_c1 = I("c1", [2, 4, 128, T], BF16)
    g_c0 = I("gc0", [2, 2, 4, 128, T], BF16)
    g_c1 = I("gc1", [2, 2, 4, 128, T], BF16)
    l_c0 = I("lc0", [2, 4, 128, T], BF16)
    l_c1 = I("lc1", [2, 4, 128, T], BF16)
    i_w8 = I("w8", [4, 8], F32)
    i_x2 = I("x2", [8, 128, T], F32)
    i_xn2 = I("xn2", [8, 128, T], BF16)
    i_xz = I("xz", [32, 128, T], BF16)
    g_xz = I("gxz", [2, 16, 128, T], BF16)
    i_yh = I("yh", [2, 16, 128, T], F32)
    g_yh = I("gyh", [2, 2, 16, 128, T], F32)
    l_y12 = I("ly12", [2, 16, 128, T], F32)
    i_x3p = I("x3p", [8, 128, T], F32)
    i_x3 = I("x3", [8, 128, T], F32)
    i_xn3 = I("xn3", [8, 128, T], BF16)
    i_xn3f = I("xn3f", [8, 128, T], F32)
    i_xz3 = I("xz3", [32, 128, T], BF16)
    g_xz3 = I("gxz3", [2, 16, 128, T], BF16)
    g_z3 = I("gz3", [2, 16, 128, T], BF16)
    l_z3 = I("lz3", [2, 8, 128, T], BF16)
    i_xact = I("xact", [8, 128, S], BF16)
    i_dbl = I("dbl", [33, S], F32)
    g_dbl = I("gdbl", [2, 33, S], F32)
    i_ap = I("ap", [2, 8, 128, T], F32)
    g_ap = I("gap", [2, 2, 8, 128, T], F32)
    l_ap = I("lap", [2, 8, 128, T], F32)
    o_x4 = nc.dram_tensor("o_x4", [8, 128, T], F32, kind="ExternalOutput").ap()

    PT = 128 * T
    o4 = rv * (4 * PT)
    o8 = rv * (8 * PT)
    o16 = rv * (16 * PT)

    p.bar()
    build_LA(S, fused=(p, dict(o_xn=i_xn0, o_q=i_q, o_v=i_v, o_gs=i_gs), "LA_"))
    AG(flat2(i_q), g_q.rearrange("j t p T -> (j t p) T"))
    AG(flat2(i_v), g_v.rearrange("j t p T -> (j t p) T"))
    SEL(l_q[:], g_q, o4, [[8 * PT, 2], [PT, 4], [T, 128], [1, T]])
    SEL(l_v[:], g_v, o4, [[8 * PT, 2], [PT, 4], [T, 128], [1, T]])
    p.bar()
    build_LB(S, fused=(p, dict(vd=l_v, qd=l_q, o_y=i_y), "LB_"))
    AG(i_y.rearrange("s t p T -> (s t p) T"), g_y.rearrange("j s t p T -> (j s t p) T"))
    SEL(l_y[:], g_y, o4, [[8 * PT, 2], [PT, 4], [T, 128], [1, T]])
    p.bar()
    build_LC1(S, fused=(p, dict(yv=l_y, gsv=i_gs, o_xp=i_xp), "LC1_"))
    p.bar()
    build_LC2(S, fused=(p, dict(xnd=i_xn0, xpd=i_xp, o_x1=i_x1, o_xn1=i_xn1,
                                o_pp=i_pp), "LC2_"))
    AG(i_pp, g_pp.rearrange("c p d -> (c p) d"), groups=ALL8)
    p.bar()
    build_LD(S, fused=(p, dict(xnd=i_xn1, ppd=g_pp,
                               o_k1=i_qkv["k1"], o_v1=i_qkv["v1"], o_q1=i_qkv["q1"],
                               o_qf=i_qkv["qf"], o_kf=i_qkv["kf"], o_vf=i_qkv["vf"],
                               o_w8=i_w8), "LD_"))
    for n in qkv_names:
        AG(flat2(i_qkv[n]), g_qkv[n].rearrange("j t p T -> (j t p) T"))
        SEL(l_qkv[n][:], g_qkv[n], o4, [[8 * PT, 2], [PT, 4], [T, 128], [1, T]])
    p.bar()
    build_LE(S, fused=(p, dict(
        k1d=l_qkv["k1"], v1d=l_qkv["v1"], q1d=l_qkv["q1"], qfd=l_qkv["qf"],
        kfd=l_qkv["kf"], vfd=l_qkv["vf"], o_c0=i_c0, o_c1=i_c1), "LE_"))
    AG(i_c0.rearrange("s t p T -> (s t p) T"), g_c0.rearrange("j s t p T -> (j s t p) T"))
    AG(i_c1.rearrange("s t p T -> (s t p) T"), g_c1.rearrange("j s t p T -> (j s t p) T"))
    SEL(l_c0[:], g_c0, o4, [[8 * PT, 2], [PT, 4], [T, 128], [1, T]])
    SEL(l_c1[:], g_c1, o4, [[8 * PT, 2], [PT, 4], [T, 128], [1, T]])
    p.bar()
    build_LF(S, fused=(p, dict(c0d=l_c0, cfd=l_c1, x1d=i_x1, w8d=i_w8,
                               o_x2=i_x2, o_xn2=i_xn2), "LF_"))
    p.bar()
    build_LG(S, fused=(p, dict(xnd=i_xn2, o_xz=i_xz), "LG_"))
    AG(i_xz[0:16].rearrange("t p T -> (t p) T"), g_xz.rearrange("j t p T -> (j t p) T"))
    p.bar()
    build_LH(S, fused=(p, dict(xzd=g_xz, o_y=i_yh), "LH_"))
    AG(i_yh.rearrange("s t p T -> (s t p) T"), g_yh.rearrange("j s t p T -> (j s t p) T"))
    SEL(l_y12[:], g_yh, o16, [[2 * 16 * PT, 2], [PT, 16], [T, 128], [1, T]])
    p.bar()
    build_LI(S, fused=(p, dict(y1d=l_y12[0], y2d=l_y12[1], zd=i_xz[16:32],
                               x2d=i_x2, o_x3p=i_x3p), "LI_"))
    p.bar()
    build_LJ(S, fused=(p, dict(xnd=i_xn2, x3pd=i_x3p, o_x3=i_x3, o_xn3=i_xn3,
                               o_xn3f=i_xn3f), "LJ_"))
    p.bar()
    build_LK(S, fused=(p, dict(xnd=i_xn3f, o_xz=i_xz3), "LK_"))
    AG(i_xz3[0:16].rearrange("t p T -> (t p) T"), g_xz3.rearrange("j t p T -> (j t p) T"))
    p.bar()
    build_LL(S, fused=(p, dict(xzd=g_xz3, o_xact=i_xact, o_dbd=i_dbl[0:1],
                               o_dbB=i_dbl[1:17], o_dbC=i_dbl[17:33]), "LL_"))
    AG(i_dbl, g_dbl.rearrange("j r T -> (j r) T"))
    AG(i_xz3[16:32].rearrange("t p T -> (t p) T"), g_z3.rearrange("j t p T -> (j t p) T"))
    SEL(l_z3[:], g_z3, o8, [[16 * PT, 2], [PT, 8], [T, 128], [1, T]])
    p.bar()
    build_LM(S, fused=(p, dict(xad=i_xact, zd=l_z3,
                               dbdd=g_dbl[:, 0:1], dbBd=g_dbl[:, 1:17],
                               dbCd=g_dbl[:, 17:33], o_ap=i_ap), "LM_"))
    AG(i_ap.rearrange("s t p T -> (s t p) T"), g_ap.rearrange("j s t p T -> (j s t p) T"))
    SEL(l_ap[:], g_ap, o8, [[2 * 8 * PT, 2], [PT, 8], [T, 128], [1, T]])
    p.bar()
    build_LN(S, fused=(p, dict(xn3d=i_xn3f, x3d=i_x3, ap0d=l_ap[0],
                               ap1d=l_ap[1], o_x4=o_x4), "LN_"))
    p.finalize()
    return nc, ctx


_FUSED_DROP = {
    "LA_": [], "LB_": ["vd", "qd"], "LC1_": ["yv", "gsv"],
    "LC2_": ["xnd", "xpd"], "LD_": ["xnd", "ppd"],
    "LE_": ["k1d", "v1d", "q1d", "qfd", "kfd", "vfd"],
    "LF_": ["c0d", "cfd", "x1d", "w8d"], "LG_": ["xnd"], "LH_": ["xzd"],
    "LI_": ["y1d", "y2d", "zd", "x2d"], "LJ_": ["xnd", "x3pd"], "LK_": ["xnd"],
    "LL_": ["xzd"], "LM_": ["xad", "zd", "dbdd", "dbBd", "dbCd"],
    "LN_": ["xn3d", "x3d", "ap0d", "ap1d"],
}


def host_FUSED(inputs, S):
    T = S // 2
    B = 4
    x0 = np.ascontiguousarray(f32c(inputs["L0_attn_x"]).transpose(0, 2, 1))
    z1 = np.zeros((B, 1024, S), np.float32)
    z2 = np.zeros((B, 2048, S), np.float32)
    z4 = np.zeros((B, 4096, S), np.float32)
    zc8 = [np.zeros((8, 128, T), np.float32) for _ in range(8)]
    zc8b = [np.zeros((8, 128, T), np.float32) for _ in range(8)]
    zpp = [np.zeros((128, 8), np.float32) for _ in range(8)]
    zw8 = np.zeros((4, 8), np.float32)
    zdb = {(b, h): np.zeros((1, S), np.float32) for b in range(B) for h in range(2)}
    zdb16 = {(b, h): np.zeros((16, S), np.float32) for b in range(B) for h in range(2)}
    zxh = {0: np.zeros((B, 1024, S), np.float32), 1: np.zeros((B, 1024, S), np.float32)}
    parts = [
        ("LA_", host_LA(inputs, x0, S)),
        ("LB_", host_LB(inputs, z1, z1, S)),
        ("LC1_", host_LC1(inputs, z1, zc8, x0, S)),
        ("LC2_", host_LC2(inputs, zc8, zc8b, S)),
        ("LD_", host_LD(inputs, zc8, zpp, S)),
        ("LE_", host_LE(inputs, z1, z1, z1, z1, z1, z1, S)),
        ("LF_", host_LF(inputs, [z.reshape(2, 4, 128, T) for z in zc8],
                        [z.reshape(2, 4, 128, T) for z in zc8], zc8, zw8, S)),
        ("LG_", host_LG(inputs, zc8, S)),
        ("LH_", host_LH(inputs, z4, S)),
        ("LI_", host_LI(inputs, [np.zeros((16, 128, T), np.float32)] * 8,
                        [np.zeros((16, 128, T), np.float32)] * 8,
                        [np.zeros((16, 128, T), np.float32)] * 8, zc8, S)),
        ("LJ_", host_LJ(inputs, zc8, zc8, S)),
        ("LK_", host_LK(inputs, zc8, S)),
        ("LL_", host_LL(inputs, z4, S)),
        ("LM_", host_LM(inputs, zxh, z2, zdb, zdb16, zdb16, S)),
        ("LN_", host_LN(inputs, zc8, zc8, zc8, zc8, S)),
    ]
    maps = [dict() for _ in range(NCORES)]
    for pfx, pm in parts:
        drop = set(_FUSED_DROP[pfx])
        for c in range(NCORES):
            for k, v in pm[c].items():
                if k in drop:
                    continue
                maps[c][pfx + k] = v
    return maps


USE_FUSED = False


def kernel_fused(**inputs):
    S = int(inputs["L0_attn_x"].shape[1])
    T = S // 2
    nc, ctx = build_FUSED(S)
    maps = host_FUSED(inputs, S)
    if RUNNER == "sim":
        from simrun import simrun
        o = simrun(nc, maps, ["o_x4"])
    else:
        global _LAUNCH_COUNT
        _LAUNCH_COUNT += 1
        res = run_bass_kernel_spmd(nc, maps, core_ids=list(range(NCORES)))
        o = [{"o_x4": np.asarray(res.results[c]["o_x4"])} for c in range(NCORES)]
    ctx.close()
    x4 = _reasm_TS(o, "o_x4", S)
    return np.ascontiguousarray(x4.transpose(0, 2, 1)).astype(np.float32)


RUNNER = "device"
_LAUNCH_COUNT = 0


def _run_launch(build_fn, host_fn, out_names, S):
    nc, ctx = build_fn(S)
    maps = host_fn()
    if RUNNER == "sim":
        from simrun import simrun
        outs = simrun(nc, maps, out_names)
    else:
        global _LAUNCH_COUNT
        _LAUNCH_COUNT += 1
        res = run_bass_kernel_spmd(nc, maps, core_ids=list(range(NCORES)))
        outs = [{k: np.asarray(res.results[c][k]) for k in out_names} for c in range(NCORES)]
    ctx.close()
    return outs


def _reasm_TS(outs, key, S, nt=8, dtype=np.float32):
    T = S // 2
    full = np.zeros((4, nt * 128, S), dtype)
    for c in range(NCORES):
        b, hh = c // 2, c % 2
        full[b, :, hh * T:(hh + 1) * T] = np.asarray(outs[c][key]).reshape(nt * 128, T).astype(dtype)
    return full


def _by_core_TS(x, S, dtype=None):
    T = S // 2
    out = []
    for c in range(NCORES):
        b, hh = c // 2, c % 2
        a = np.ascontiguousarray(x[b, :, hh * T:(hh + 1) * T]).reshape(-1, 128, T)
        if dtype is not None:
            a = a.astype(dtype)
        out.append(a)
    return out


def kernel(**inputs):
    if USE_FUSED:
        return kernel_fused(**inputs)
    import ml_dtypes
    BF = ml_dtypes.bfloat16
    S = int(inputs["L0_attn_x"].shape[1])
    T = S // 2
    x0 = np.ascontiguousarray(f32c(inputs["L0_attn_x"]).transpose(0, 2, 1))

    o = _run_launch(build_LA, lambda: host_LA(inputs, x0, S),
                    ["o_xn", "o_q", "o_v", "o_gs"], S)
    xn0_by_core = [np.asarray(c_["o_xn"]).reshape(8, 128, T) for c_ in o]
    qf_ = _reasm_TS(o, "o_q", S)
    vf_ = _reasm_TS(o, "o_v", S)
    gs_by_core = [np.asarray(c_["o_gs"]).reshape(8, 128, T) for c_ in o]

    o = _run_launch(build_LB, lambda: host_LB(inputs, qf_, vf_, S), ["o_y"], S)
    y_full = np.zeros((4, 1024, S), np.float32)
    for c in range(NCORES):
        b, g = c // 2, c % 2
        oy = np.asarray(o[c]["o_y"]).reshape(2, 512, T)
        for j in range(2):
            y_full[b, g * 512:(g + 1) * 512, j * T:(j + 1) * T] = oy[j]
    del qf_, vf_

    o = _run_launch(build_LC1, lambda: host_LC1(inputs, y_full, gs_by_core, x0, S),
                    ["o_xp"], S)
    xp_by_core = [np.asarray(c_["o_xp"]).reshape(8, 128, T) for c_ in o]
    del y_full, gs_by_core

    o = _run_launch(build_LC2, lambda: host_LC2(inputs, xn0_by_core, xp_by_core, S),
                    ["o_x1", "o_xn1", "o_pp"], S)
    x1_by_core = [np.asarray(c_["o_x1"]).reshape(8, 128, T) for c_ in o]
    xn1_by_core = [np.asarray(c_["o_xn1"]).reshape(8, 128, T) for c_ in o]
    pp_by_core = [np.asarray(c_["o_pp"]).reshape(128, 8) for c_ in o]
    del xn0_by_core, xp_by_core

    o = _run_launch(build_LD, lambda: host_LD(inputs, xn1_by_core, pp_by_core, S),
                    ["o_k1", "o_v1", "o_q1", "o_qf", "o_kf", "o_vf", "o_w8"], S)
    k1 = _reasm_TS(o, "o_k1", S)
    v1 = _reasm_TS(o, "o_v1", S)
    q1 = _reasm_TS(o, "o_q1", S)
    qf = _reasm_TS(o, "o_qf", S)
    kf = _reasm_TS(o, "o_kf", S)
    vf = _reasm_TS(o, "o_vf", S)
    w8 = np.asarray(o[0]["o_w8"], np.float32).reshape(4, 8)

    o = _run_launch(build_LE, lambda: host_LE(inputs, k1, v1, q1, qf, kf, vf, S),
                    ["o_c0", "o_c1"], S)
    c0 = np.zeros((4, 1024, S), np.float32)
    cf = np.zeros((4, 1024, S), np.float32)
    for c in range(NCORES):
        b, hg = c // 2, c % 2
        csl = slice(hg * 512, (hg + 1) * 512)
        a0 = np.asarray(o[c]["o_c0"], np.float32).reshape(2, 512, T)
        a1 = np.asarray(o[c]["o_c1"], np.float32).reshape(2, 512, T)
        for j in range(2):
            c0[b, csl, j * T:(j + 1) * T] = a0[j]
            cf[b, csl, j * T:(j + 1) * T] = a1[j]
    del k1, v1, q1, qf, kf, vf

    o = _run_launch(build_LF, lambda: host_LF(
        inputs, _by_core_TS(c0, S, BF), _by_core_TS(cf, S, BF), x1_by_core, w8, S),
        ["o_x2", "o_xn2"], S)
    x2_by_core = [np.asarray(c_["o_x2"]).reshape(8, 128, T) for c_ in o]
    xn2_by_core = [np.asarray(c_["o_xn2"]).reshape(8, 128, T) for c_ in o]
    del c0, cf, x1_by_core

    o = _run_launch(build_LG, lambda: host_LG(inputs, xn2_by_core, S), ["o_xz"], S)
    xz = _reasm_TS(o, "o_xz", S, nt=32)

    o = _run_launch(build_LH, lambda: host_LH(inputs, xz, S), ["o_y"], S)
    y1 = np.zeros((4, 2048, S), np.float32)
    y2 = np.zeros((4, 2048, S), np.float32)
    for c in range(NCORES):
        b, br = c // 2, c % 2
        oy = np.asarray(o[c]["o_y"]).reshape(2, 2048, T)
        tgt = y1 if br == 0 else y2
        for j in range(2):
            tgt[b, :, j * T:(j + 1) * T] = oy[j]

    o = _run_launch(build_LI, lambda: host_LI(
        inputs, _by_core_TS(y1, S), _by_core_TS(y2, S),
        _by_core_TS(xz[:, 2048:4096], S, BF), x2_by_core, S), ["o_x3p"], S)
    x3p_by_core = [np.asarray(c_["o_x3p"]).reshape(8, 128, T) for c_ in o]
    del y1, y2, xz, x2_by_core

    o = _run_launch(build_LJ, lambda: host_LJ(inputs, xn2_by_core, x3p_by_core, S),
                    ["o_x3", "o_xn3", "o_xn3f"], S)
    x3_by_core = [np.asarray(c_["o_x3"]).reshape(8, 128, T) for c_ in o]
    xn3f_by_core = [np.asarray(c_["o_xn3f"]).reshape(8, 128, T) for c_ in o]
    del xn2_by_core, x3p_by_core

    o = _run_launch(build_LK, lambda: host_LK(inputs, xn3f_by_core, S), ["o_xz"], S)
    xz3 = _reasm_TS(o, "o_xz", S, nt=32)

    o = _run_launch(build_LL, lambda: host_LL(inputs, xz3, S),
                    ["o_xact", "o_dbd", "o_dbB", "o_dbC"], S)
    xact_by_half = {0: np.zeros((4, 1024, S), np.float32), 1: np.zeros((4, 1024, S), np.float32)}
    dbd, dbB, dbC = {}, {}, {}
    for c in range(NCORES):
        b, hf = c // 2, c % 2
        xact_by_half[hf][b] = np.asarray(o[c]["o_xact"]).reshape(1024, S)
        dbd[(b, hf)] = np.asarray(o[c]["o_dbd"]).reshape(1, S)
        dbB[(b, hf)] = np.asarray(o[c]["o_dbB"]).reshape(16, S)
        dbC[(b, hf)] = np.asarray(o[c]["o_dbC"]).reshape(16, S)

    o = _run_launch(build_LM, lambda: host_LM(
        inputs, xact_by_half, xz3[:, 2048:4096], dbd, dbB, dbC, S), ["o_ap"], S)
    ap0 = np.zeros((4, 1024, S), np.float32)
    ap1 = np.zeros((4, 1024, S), np.float32)
    for c in range(NCORES):
        b, hf = c // 2, c % 2
        oa = np.asarray(o[c]["o_ap"]).reshape(2, 1024, T)
        tgt = ap0 if hf == 0 else ap1
        for j in range(2):
            tgt[b, :, j * T:(j + 1) * T] = oa[j]
    del xz3, xact_by_half

    o = _run_launch(build_LN, lambda: host_LN(
        inputs, xn3f_by_core, x3_by_core,
        _by_core_TS(ap0.transpose(0, 2, 1).transpose(0, 2, 1), S) if False else
        _by_core_TS(ap0, S), _by_core_TS(ap1, S), S), ["o_x4"], S)
    x4 = _reasm_TS(o, "o_x4", S)
    return np.ascontiguousarray(x4.transpose(0, 2, 1)).astype(np.float32)
